# version 12
# baseline (speedup 1.0000x reference)
import numpy as np
import ml_dtypes
from contextlib import ExitStack
import concourse.bass as bass
import concourse.mybir as mybir
from concourse.bass_utils import run_bass_kernel_spmd

F32 = mybir.dt.float32
BF16 = mybir.dt.bfloat16
AF = mybir.ActivationFunctionType
ALU = mybir.AluOpType

NCORES = 8
D, T, CH, TGN, TG, TTN = 2048, 2048, 16, 4, 512, 16
NF, NFC, NH, FB, FT = 8192, 64, 16, 4, 512
EPS = 1e-5
VOFF = {'bqk': 0, 'bv': 16, 'bu': 32, 'bo': 48, 'g': 64, 'b': 80, 'b1': 96, 'b0': 112}
NV = 176

def pack_vecs(ins):
    v = np.zeros((128, NV), np.float32)
    def put(name, arr):
        a = np.asarray(arr, np.float32).reshape(-1, 128).T
        v[:, VOFF[name]:VOFF[name] + a.shape[1]] = a
    put('bqk', ins['L0_attn_bqk']); put('bv', ins['L0_attn_bv'])
    put('bu', ins['L0_attn_bu']); put('bo', ins['L0_attn_bo'])
    put('g', ins['L0_attn_ln_g']); put('b', ins['L0_attn_ln_b'])
    put('b1', ins['L0_ffn_bias1']); put('b0', ins['L0_ffn_bias0'])
    return v

class Sem:
    def __init__(self, nc, es, name):
        self.h = es.enter_context(nc.semaphore(name))
        self.n = 0
        self.name = name
    def bump(self, k=1):
        self.n += k
        return self.n

class Ring:
    def __init__(self, nslots):
        self.marks = [[] for _ in range(nslots)]
        self.ns = nslots
    def prod_wait(self, i):
        return list(self.marks[i % self.ns])
    def consumed(self, i, *marks):
        self.marks[i % self.ns] = [m for m in marks if m is not None]
    def seed_all(self, *marks):
        for i in range(self.ns):
            self.marks[i] = [m for m in marks if m is not None]

def build_nc(dbg=False, stop_after=None, simple_elu=False, elu_mode=4):
    nc = bass.Bass("TRN2", target_bir_lowering=False, detect_race_conditions=False)
    es = ExitStack()

    xT   = nc.dram_tensor("xT",   [D, T], F32, kind="ExternalInput").ap()
    wqk  = nc.dram_tensor("wqk",  [D, D], F32, kind="ExternalInput").ap()
    wv   = nc.dram_tensor("wv",   [D, D], F32, kind="ExternalInput").ap()
    wu   = nc.dram_tensor("wu",   [D, D], F32, kind="ExternalInput").ap()
    wo   = nc.dram_tensor("wo",   [D, D], F32, kind="ExternalInput").ap()
    w0t  = nc.dram_tensor("w0t",  [D, NF], F32, kind="ExternalInput").ap()
    w1t  = nc.dram_tensor("w1t",  [NF, D], F32, kind="ExternalInput").ap()
    vecs = nc.dram_tensor("vecs", [128, NV], F32, kind="ExternalInput").ap()
    grow = nc.dram_tensor("grow", [2, D], F32, kind="ExternalInput").ap()
    idnt = nc.dram_tensor("idnt", [128, 128], F32, kind="ExternalInput").ap()
    onesr = nc.dram_tensor("onesr", [1, TG], F32, kind="ExternalInput").ap()
    yT   = nc.dram_tensor("yT",   [D, T], F32, kind="ExternalOutput").ap()

    q_fm_sp = nc.dram_tensor("q_fm_sp", [D, T], BF16, kind="Internal").ap()
    q_tm_sp = nc.dram_tensor("q_tm_sp", [D, T], BF16, kind="Internal").ap()
    u_sp    = nc.dram_tensor("u_sp",    [D, T], BF16, kind="Internal").ap()
    y1_sp   = nc.dram_tensor("y1_sp",   [D, T], F32, kind="Internal").ap()
    w0c     = nc.dram_tensor("w0c", [D, NF], BF16, kind="Internal").ap()
    wo_c    = nc.dram_tensor("wo_c", [D, D], BF16, kind="Internal").ap()
    w1c     = nc.dram_tensor("w1c", [NF, D], BF16, kind="Internal").ap()
    cc_in   = nc.dram_tensor("cc_in",  [128, NH * 128], F32, kind="Internal").ap()
    cc_out  = nc.dram_tensor("cc_out", [128, NH * 128], F32, kind="Internal", addr_space="Shared").ap()

    dbg_t = {}
    if dbg:
        for nm in ["dx1", "dq", "dv", "du", "dout", "dz", "da"]:
            dbg_t[nm] = nc.dram_tensor(nm, [D, T], BF16, kind="ExternalOutput").ap()
        dbg_t["dy1"] = nc.dram_tensor("dy1", [D, T], F32, kind="ExternalOutput").ap()
        dbg_t["dx2"] = nc.dram_tensor("dx2", [128, 8192], BF16, kind="ExternalOutput").ap()
        dbg_t["dD"] = nc.dram_tensor("dD", [128, 32768], BF16, kind="ExternalOutput").ap()
        dbg_t["dE"] = nc.dram_tensor("dE", [128, 8192], BF16, kind="ExternalOutput").ap()
        dbg_t["dkv"] = nc.dram_tensor("dkv", [128, NH * 128], F32, kind="ExternalOutput").ap()

    MB = 24576
    def at(name, shape, dt, off):
        return nc.alloc_sbuf_tensor_at(name, shape, dt, offset=MB + off).ap()

    A1   = at("A1", [128, 32768], BF16, 0)
    A2   = at("A2", [128, 32768], BF16, 65536)
    xrng = [at(f"xr{i}", [128, 1024], F32, 131072 + i * 4096) for i in range(2)]
    kvacc = at("kvacc", [128, 2048], F32, 139264)
    y1rng = [at(f"y1r{i}", [128, 1024], F32, 139264 + i * 4096) for i in range(2)]
    escr  = [at(f"escr{i}", [128, 512], F32, 139264 + i * 2048) for i in range(2)]
    fy1rng = [at(f"fy1r{i}", [128, 512], F32, 139264 + i * 2048) for i in range(2)]
    fy2rng = [at(f"fy2r{i}", [128, 512], F32, 143360 + i * 2048) for i in range(2)]
    g_row = at("g_row", [1, D], BF16, 147456)
    b_row = at("b_row", [1, D], BF16, 151552)
    rd_bf = at("rd_bf", [1, D], BF16, 131072)
    sd_bf = at("sd_bf", [1, D], BF16, 135168)
    wrng  = [at(f"wr{i}", [128, 2048], BF16, 155648 + i * 4096) for i in range(2)]
    qtmr  = [at(f"qtm{i}", [128, 2048], BF16, 163840 + i * 4096) for i in range(2)]
    kv_bf = at("kv_bf", [128, 2048], BF16, 155648)
    qstr  = [at(f"qst{i}", [128, 2048], BF16, 159744 + i * 4096) for i in range(2)]
    w0rng = [at(f"w0r{i}", [128, 2048], BF16, 155648 + i * 4096) for i in range(4)]
    w1rng = at("w1r", [128, 4096], BF16, 172032)
    ctscr = at("ctscr", [128, 2048], BF16, 163840)
    urng  = [at(f"ur{i}", [128, 2048], BF16, 172032 + i * 4096) for i in range(2)]
    rBt   = at("rBt", [128, 2048], BF16, 172032)
    vtmr  = [at(f"vtm{i}", [128, 2048], BF16, 172032 + i * 4096) for i in range(2)]
    sqr   = [at(f"sq{i}", [128, 512], BF16, 180224 + i * 1024) for i in range(2)]
    s1t  = at("s1t", [1, 512], F32, 182272)
    s2t  = at("s2t", [1, 512], F32, 184320)
    tmpt = at("tmpt", [1, 512], F32, 186368)
    r_bf = at("r_bf", [1, D], BF16, 188416)
    s_bf = at("s_bf", [1, D], BF16, 192512)
    vsb   = at("vsb", [128, NV], F32, 196608)
    nvsb  = at("nvsb", [128, 16], F32, 197312)
    idbf  = at("idbf", [128, 128], BF16, 197376)
    onesc = at("onesc", [128, 1], BF16, 197632)
    ones_row = at("ones_row", [1, TG], BF16, 197664)
    scrA  = [at(f"scrA{i}", [128, 512], BF16, 198688 + i * 1024) for i in range(2)]
    rdBb  = at("rdBb", [128, 512], BF16, 202784)
    sdBb  = at("sdBb", [128, 512], BF16, 147456)
    scrB  = [at(f"scrB{i}", [128, 512], BF16, 200736 + i * 1024) for i in range(2)]

    ps = lambda name, shape, dt=F32: es.enter_context(nc.psum_tensor(name, shape, dt))
    PA, PB, PC, PD = (ps(f"P{i}", [128, 512]) for i in "ABCD")
    PS1, PS2 = ps("PS1", [128, 512]), ps("PS2", [128, 512])
    PT1, PT2 = ps("PT1", [128, 1024], BF16), ps("PT2", [128, 1024], BF16)
    PROT = [PA, PB, PC, PD]

    S = lambda nm: Sem(nc, es, nm)
    cT, cV, cS = S("cT"), S("cV"), S("cS")
    CENG = {'tensor': cT, 'vector': cV, 'scalar': cS}
    nG, nY = S("nG"), S("nY")
    sAR = S("sAR")
    sG, sQFS, sDBG = S("sG"), S("sQFS"), S("sDBG")
    slot_sems = {}
    def slot_sem(name, j):
        if (name, j) not in slot_sems:
            slot_sems[(name, j)] = S(f"{name}{j}")
        return slot_sems[(name, j)]

    q = {e: [] for e in ("tensor", "vector", "scalar", "gpsimd", "sync")}
    phase_snaps = {}
    def snap(name):
        phase_snaps[name] = {e: len(q[e]) for e in q}
    def on(eng, fn):
        q[eng].append(fn)
    def wait(eng, sem, val):
        if val and val > 0:
            h = sem.h if isinstance(sem, Sem) else sem
            on(eng, lambda e, h=h, val=val: e.wait_ge(h, val))
    def wait_mark(eng, marks):
        for mk in (marks or []):
            if mk is not None:
                wait(eng, mk[0], mk[1])
    def sig(eng, fn):
        c = CENG[eng]
        n = c.bump()
        def w(e, fn=fn, c=c):
            fn(e).then_inc(c.h, 1)
        on(eng, w)
        return (c, n)
    def notify(eng, sem):
        n = sem.bump()
        on(eng, lambda e, sem=sem: e.nop().then_inc(sem.h, 1))
        return (sem, n)
    def dma(eng, sem, out_ap, in_ap):
        n = sem.bump(16)
        on(eng, lambda e, sem=sem, out_ap=out_ap, in_ap=in_ap:
            e.dma_start(out=out_ap, in_=in_ap).then_inc(sem.h, 16))
        return (sem, n)

    mmM = {}
    evM = {id(p): [] for p in (PA, PB, PC, PD, PS1, PS2, PT1, PT2)}
    def use_bank(p):
        wait_mark('tensor', evM[id(p)])
        evM[id(p)] = []

    SQ4 = [sqr[0], sqr[1], scrA[0], scrA[1]]
    for dst, src in [(vsb, vecs), (idbf, idnt), (g_row, grow[0:1, :]),
                     (b_row, grow[1:2, :]), (ones_row, onesr)]:
        dma('gpsimd', sG, dst, src)
    x_marks = []
    for c in range(CH):
        sx = slot_sem("XLD", c % 4)
        if c >= 4:
            wait('gpsimd', sx, sx.n)
        x_marks.append(dma('gpsimd', sx, A1[:, c * 2048:(c + 1) * 2048], xT[c * 128:(c + 1) * 128, :]))
    wait('gpsimd', sG, sG.n)
    for j in range(4):
        wait('gpsimd', slot_sem("XLD", j), slot_sem("XLD", j).n)
    GXRD = notify('gpsimd', nG)

    on('vector', lambda v: v.memset(onesc[:, :], 1.0))
    wait_mark('vector', [GXRD])
    CONSTS_RDY = sig('vector', lambda v: v.tensor_scalar(
        nvsb[:, :], vsb[:, VOFF['bqk']:VOFF['bqk'] + 16], -1.0, None, ALU.mult))

    snap("p1")
    sq_ring = Ring(4)
    sq_it = [0]
    def stats(src_ap_fn, tg_list, tgw, rdst, sdst, src_ready):
        rows_mark = None
        for tg in tg_list:
            use_bank(PS1); use_bank(PS2)
            for c in range(CH):
                it = sq_it[0]; sq_it[0] += 1
                slot = SQ4[it % 4]
                wait_mark('scalar', sq_ring.prod_wait(it))
                if src_ready:
                    wait_mark('scalar', [src_ready(c)])
                msq = sig('scalar', lambda s, slot=slot, ap=src_ap_fn(c, tg), tgw=tgw:
                          s.square(slot[:, :tgw], ap))
                if src_ready:
                    wait_mark('tensor', [src_ready(c)])
                on('tensor', lambda t, ap=src_ap_fn(c, tg), c=c, tgw=tgw: t.matmul(
                    PS1[0:1, :tgw], onesc[:, :], ap, start=(c == 0), stop=(c == CH - 1)))
                wait_mark('tensor', [msq])
                mc = sig('tensor', lambda t, slot=slot, c=c, tgw=tgw: t.matmul(
                    PS2[0:1, :tgw], onesc[:, :], slot[:, :tgw],
                    start=(c == 0), stop=(c == CH - 1)))
                sq_ring.consumed(it, mc)
                if c == CH - 1:
                    mm_stats = mc
            wait_mark('vector', [mm_stats])
            e1 = sig('vector', lambda v, tgw=tgw: v.tensor_copy(s1t[:, :tgw], PS1[0:1, :tgw]))
            e2 = sig('vector', lambda v, tgw=tgw: v.tensor_copy(s2t[:, :tgw], PS2[0:1, :tgw]))
            evM[id(PS1)].append(e1)
            evM[id(PS2)].append(e2)
            on('vector', lambda v, tgw=tgw: v.tensor_scalar(s1t[:, :tgw], s1t[:, :tgw], 1.0 / D, None, ALU.mult))
            on('vector', lambda v, tgw=tgw: v.tensor_tensor(tmpt[:, :tgw], s1t[:, :tgw], s1t[:, :tgw], ALU.mult))
            on('vector', lambda v, tgw=tgw: v.tensor_scalar(s2t[:, :tgw], s2t[:, :tgw], 1.0 / D, EPS, ALU.mult, ALU.add))
            on('vector', lambda v, tgw=tgw: v.tensor_tensor(s2t[:, :tgw], s2t[:, :tgw], tmpt[:, :tgw], ALU.subtract))
            mrec = sig('vector', lambda v, tgw=tgw: v.reciprocal(s2t[:, :tgw], s2t[:, :tgw]))
            wait_mark('scalar', [mrec])
            mr = sig('scalar', lambda s, tg=tg, tgw=tgw: s.activation(
                rdst[0:1, tg * tgw:(tg + 1) * tgw], s2t[:, :tgw], AF.Sqrt))
            wait_mark('vector', [mr])
            rows_mark = sig('vector', lambda v, tg=tg, tgw=tgw: v.scalar_tensor_tensor(
                sdst[0:1, tg * tgw:(tg + 1) * tgw], s1t[:, :tgw], -1.0,
                rdst[0:1, tg * tgw:(tg + 1) * tgw], ALU.mult, ALU.mult))
        return rows_mark

    def bcast(p, row_ap, n, extra=None):
        use_bank(p)
        wait_mark('tensor', [extra] if extra else [])
        m = sig('tensor', lambda t, p=p, row_ap=row_ap, n=n: t.matmul(
            p[:, :n], ones_row[0:1, 0:128], row_ap, start=True, stop=True))
        mmM[id(p)] = m
        return m

    wait_mark('tensor', [CONSTS_RDY])

    def xsrc(c, tg):
        return A1[:, c * 2048 + tg * TG: c * 2048 + (tg + 1) * TG]
    ROWS_A = stats(xsrc, range(TGN), TG, r_bf, s_bf, src_ready=lambda c: x_marks[c])
    wait_mark('vector', [GXRD])
    x1_done = {}
    for tg in range(TGN):
        mb1 = bcast(PS1, r_bf[0:1, tg * TG:(tg + 1) * TG], TG, extra=ROWS_A)
        mb2 = bcast(PS2, s_bf[0:1, tg * TG:(tg + 1) * TG], TG)
        wait_mark('vector', [mb1])
        e1 = sig('vector', lambda v, tg=tg: v.tensor_copy(rBt[:, tg * TG:(tg + 1) * TG], PS1[:, :TG]))
        evM[id(PS1)].append(e1)
        wait_mark('vector', [mb2])
        last_e = None
        for c in range(CH):
            src = A1[:, c * 2048 + tg * TG: c * 2048 + (tg + 1) * TG]
            dst = A2[:, c * 2048 + tg * TG: c * 2048 + (tg + 1) * TG]
            on('vector', lambda v, src=src, dst=dst, tg=tg: v.tensor_tensor(
                dst, src, rBt[:, tg * TG:(tg + 1) * TG], ALU.mult))
            m = sig('vector', lambda v, dst=dst: v.tensor_tensor(dst, dst, PS2[:, :TG], ALU.add))
            x1_done[(c, tg)] = m
            last_e = m
        evM[id(PS2)].append(last_e)
    X1_ALL = x1_done[(CH - 1, TGN - 1)]

    if dbg:
        for c in range(CH):
            wait_mark('sync', [X1_ALL])
            dma('sync', sDBG, dbg_t["dx1"][c * 128:(c + 1) * 128, :], A2[:, c * 2048:(c + 1) * 2048])

    snap("p3")
    wslot_ring = Ring(2)
    wslot_it = [0]
    def projfm(wdram, mov_ap, mov_wait, evac, prefix, slots=None, ring=None, itc=None,
               after_load=None, ld_eng='gpsimd', ld_sem='WL', pre_wait=None):
        slots = slots or wrng
        ring = ring if ring is not None else wslot_ring
        itc = itc if itc is not None else wslot_it
        nsl = len(slots)
        for oc in range(CH):
            it = itc[0]; itc[0] += 1
            j = it % nsl
            slot = slots[j]
            ssem = slot_sem(ld_sem, j)
            if oc == 0 and pre_wait is not None:
                wait_mark(ld_eng, [pre_wait])
            wait_mark(ld_eng, ring.prod_wait(it))
            src = wdram[:, oc * 128:(oc + 1) * 128].rearrange("(c p) n -> p c n", p=128)
            mld = dma(ld_eng, ssem, slot[:, :].rearrange("p (c n) -> p c n", n=128), src)
            if after_load is not None:
                after_load(oc)
            wait_mark('tensor', [mld])
            if mov_wait is not None and oc == 0:
                wait_mark('tensor', [mov_wait])
            mlast = None
            for tg in range(TGN):
                p = PROT[(oc * TGN + tg) % 4]
                use_bank(p)
                for k in range(CH - 1):
                    on('tensor', lambda t, p=p, slot=slot, k=k, tg=tg: t.matmul(
                        p[:, :], slot[:, k * 128:(k + 1) * 128], mov_ap(k, tg),
                        start=(k == 0), stop=False))
                mlast = sig('tensor', lambda t, p=p, slot=slot, tg=tg: t.matmul(
                    p[:, :], slot[:, (CH - 1) * 128:CH * 128], mov_ap(CH - 1, tg),
                    start=False, stop=True))
                mmM[id(p)] = mlast
                evac(oc, tg, p)
            ring.consumed(it, mlast)

    q_done = {}
    e_ring = Ring(2)
    e_it = [0]
    def elu_evac(oc, tg, p):
        mmn = mmM[id(p)]
        it = e_it[0]; e_it[0] += 1
        ms = scrA[it % 2]; esl = escr[it % 2]; rs = scrB[it % 2]
        nbq = nvsb[:, oc:oc + 1]
        bq = vsb[:, VOFF['bqk'] + oc:VOFF['bqk'] + oc + 1]
        if oc == 0 and tg == 0:
            wait_mark('scalar', [CONSTS_RDY])
        wait_mark('scalar', e_ring.prod_wait(it))
        wait_mark('scalar', [mmn])
        on('scalar', lambda s, p=p, ms=ms, nbq=nbq: s.activation(ms[:, :], p[:, :], AF.Relu, bias=nbq, scale=-1.0))
        if elu_mode >= 2:
            me = sig('scalar', lambda s, ms=ms, esl=esl: s.activation(esl[:, :], ms[:, :], AF.Exp, scale=-1.0))
        else:
            me = sig('scalar', lambda s, ms=ms, esl=esl: s.activation(esl[:, :], ms[:, :], AF.Copy))
        evM[id(p)].append(me)
        wait_mark('vector', [me])
        if elu_mode >= 3:
            mrs = sig('vector', lambda v, p=p, rs=rs, bq=bq: v.tensor_scalar(rs[:, :], p[:, :], bq, 0.0, ALU.add, ALU.max))
        else:
            mrs = sig('vector', lambda v, p=p, rs=rs: v.tensor_copy(rs[:, :], p[:, :]))
        evM[id(p)].append(mrs)
        dst = A1[:, oc * 2048 + tg * TG: oc * 2048 + (tg + 1) * TG]
        if elu_mode >= 4:
            mq = sig('vector', lambda v, rs=rs, esl=esl, dst=dst: v.scalar_tensor_tensor(
                dst, rs, -1.0, esl, ALU.add, ALU.add))
        else:
            mq = sig('vector', lambda v, rs=rs, dst=dst: v.tensor_copy(dst, rs[:, :]))
        q_done[(oc, tg)] = mq
        e_ring.consumed(it, mq)

    def simple_evac(oc, tg, p):
        mmn = mmM[id(p)]
        dst = A1[:, oc * 2048 + tg * TG: oc * 2048 + (tg + 1) * TG]
        wait_mark('scalar', [mmn])
        mact = sig('scalar', lambda s, p=p, dst=dst: s.activation(dst, p[:, :], AF.Copy))
        evM[id(p)].append(mact)
        q_done[(oc, tg)] = mact
    projfm(wqk, lambda k, tg: A2[:, k * 2048 + tg * TG: k * 2048 + (tg + 1) * TG],
           X1_ALL, simple_evac if simple_elu else elu_evac, "qk")
    if dbg:
        for c in range(CH):
            wait_mark('sync', [q_done[(c, tg)] for tg in range(TGN)])
            dma('sync', sDBG, dbg_t["dq"][c * 128:(c + 1) * 128, :], A1[:, c * 2048:(c + 1) * 2048])

    snap("p4")
    qtm_ring = Ring(2)
    for h in range(NH):
        pt = PT1 if h % 2 == 0 else PT2
        stage = qtmr[h % 2]
        wait_mark('tensor', [q_done[(h, tg)] for tg in range(TGN)])
        wait_mark('vector', qtm_ring.prod_wait(h))
        mcopy = None
        for half in range(2):
            use_bank(pt)
            for j in range(7):
                tt = half * 8 + j
                src = A1[:, h * 2048 + tt * 128: h * 2048 + (tt + 1) * 128]
                on('tensor', lambda t, pt=pt, j=j, src=src: t.transpose(
                    pt[:, j * 128:(j + 1) * 128], src, idbf[:, :]))
            tt = half * 8 + 7
            src = A1[:, h * 2048 + tt * 128: h * 2048 + (tt + 1) * 128]
            mtp = sig('tensor', lambda t, pt=pt, src=src: t.transpose(
                pt[:, 7 * 128:8 * 128], src, idbf[:, :]))
            wait_mark('vector', [mtp])
            mcopy = sig('vector', lambda v, pt=pt, stage=stage, half=half: v.tensor_copy(
                stage[:, half * 1024:(half + 1) * 1024], pt[:, :]))
            evM[id(pt)].append(mcopy)
        wait_mark('sync', [mcopy])
        ssp = slot_sem("SPQ", h % 2)
        msp = dma('sync', ssp, q_tm_sp[h * 128:(h + 1) * 128, :], stage[:, :])
        qtm_ring.consumed(h, msp)
    for c in range(CH):
        wait_mark('sync', [q_done[(c, tg)] for tg in range(TGN)])
        dma('sync', sQFS, q_fm_sp[c * 128:(c + 1) * 128, :], A1[:, c * 2048:(c + 1) * 2048])
    wait('sync', sQFS, sQFS.n)
    Y_QFM = notify('sync', nY)

    snap("p5")

    snap("p6")
    v_done = {}
    def v_evac(oc, tg, p):
        mmn = mmM[id(p)]
        bv = vsb[:, VOFF['bv'] + oc:VOFF['bv'] + oc + 1]
        dst = A1[:, oc * 2048 + tg * TG: oc * 2048 + (tg + 1) * TG]
        wait_mark('scalar', [mmn])
        mact = sig('scalar', lambda s, p=p, dst=dst, bv=bv: s.activation(dst, p[:, :], AF.Silu, bias=bv))
        evM[id(p)].append(mact)
        v_done[(oc, tg)] = mact
    projfm(wv, lambda k, tg: A2[:, k * 2048 + tg * TG: k * 2048 + (tg + 1) * TG],
           Y_QFM, v_evac, "v")
    if dbg:
        for c in range(CH):
            wait_mark('sync', [v_done[(c, tg)] for tg in range(TGN)])
            dma('sync', sDBG, dbg_t["dv"][c * 128:(c + 1) * 128, :], A1[:, c * 2048:(c + 1) * 2048])

    snap("p7")
    vtm_ring = Ring(2)
    vtm_ring.marks[0] = [X1_ALL]
    qtm_ld_ring = Ring(2)
    wait('sync', slot_sem("SPQ", 0), slot_sem("SPQ", 0).n)
    wait('sync', slot_sem("SPQ", 1), slot_sem("SPQ", 1).n)
    kv_done = []
    for h in range(NH):
        pt = PT1 if h % 2 == 0 else PT2
        vstage = vtmr[h % 2]
        qslot = qtmr[h % 2]
        wait_mark('sync', qtm_ld_ring.prod_wait(h))
        sql = slot_sem("QTL", h % 2)
        mq = dma('sync', sql, qslot[:, :], q_tm_sp[h * 128:(h + 1) * 128, :])
        wait_mark('tensor', [v_done[(h, tg)] for tg in range(TGN)])
        wait_mark('vector', vtm_ring.prod_wait(h))
        for half in range(2):
            use_bank(pt)
            for j in range(7):
                tt = half * 8 + j
                src = A1[:, h * 2048 + tt * 128: h * 2048 + (tt + 1) * 128]
                on('tensor', lambda t, pt=pt, j=j, src=src: t.transpose(
                    pt[:, j * 128:(j + 1) * 128], src, idbf[:, :]))
            tt = half * 8 + 7
            src = A1[:, h * 2048 + tt * 128: h * 2048 + (tt + 1) * 128]
            mtp = sig('tensor', lambda t, pt=pt, src=src: t.transpose(
                pt[:, 7 * 128:8 * 128], src, idbf[:, :]))
            wait_mark('vector', [mtp])
            mcp = sig('vector', lambda v, pt=pt, vstage=vstage, half=half: v.tensor_copy(
                vstage[:, half * 1024:(half + 1) * 1024], pt[:, :]))
            evM[id(pt)].append(mcp)
        p = PROT[h // 4]
        if h % 4 == 0:
            use_bank(p)
        wait_mark('tensor', [mcp, mq])
        for tt in range(TTN - 1):
            on('tensor', lambda t, p=p, h=h, tt=tt, qslot=qslot, vstage=vstage: t.matmul(
                p[:, (h % 4) * 128:(h % 4 + 1) * 128],
                qslot[:, tt * 128:(tt + 1) * 128],
                vstage[:, tt * 128:(tt + 1) * 128],
                start=(tt == 0), stop=False))
        mkv = sig('tensor', lambda t, p=p, h=h, qslot=qslot, vstage=vstage: t.matmul(
            p[:, (h % 4) * 128:(h % 4 + 1) * 128],
            qslot[:, (TTN - 1) * 128:TTN * 128],
            vstage[:, (TTN - 1) * 128:TTN * 128],
            start=False, stop=True))
        kv_done.append(mkv)
        qtm_ld_ring.consumed(h, mkv)
        vtm_ring.consumed(h, mkv)
    KV_ALL = kv_done[-1]
    wait_mark('vector', [KV_ALL])
    mevkv = None
    for i in range(4):
        mevkv = sig('vector', lambda v, i=i: v.tensor_copy(kvacc[:, i * 512:(i + 1) * 512], PROT[i][:, :]))
        evM[id(PROT[i])].append(mevkv)
    wait_mark('gpsimd', [mevkv])
    dma('gpsimd', sG, cc_in[:, :], kvacc[:, :])
    wait('gpsimd', sG, sG.n)
    on('gpsimd', lambda g: g.collective_compute(
        "AllReduce", ALU.add, replica_groups=[list(range(NCORES))],
        ins=[cc_in[:, :]], outs=[cc_out[:, :]]).then_inc(sAR.h, 1))
    sAR.bump()
    sPWO = Sem(nc, es, "sPWO")
    u_stage_ring = Ring(2)
    u_stage_ring.seed_all(KV_ALL)
    u_sp_marks = []
    def u_evac(oc, tg, p):
        mmn = mmM[id(p)]
        stage = urng[oc % 2]
        bu = vsb[:, VOFF['bu'] + oc:VOFF['bu'] + oc + 1]
        if tg == 0:
            wait_mark('scalar', u_stage_ring.prod_wait(oc))
        wait_mark('scalar', [mmn])
        mact = sig('scalar', lambda s, p=p, stage=stage, tg=tg, bu=bu: s.activation(
            stage[:, tg * TG:(tg + 1) * TG], p[:, :], AF.Silu, bias=bu))
        evM[id(p)].append(mact)
        if tg == TGN - 1:
            wait_mark('sync', [mact])
            ssp = slot_sem("SPU", oc % 2)
            msp = dma('sync', ssp, u_sp[oc * 128:(oc + 1) * 128, :], stage[:, :])
            u_sp_marks.append(msp)
            u_stage_ring.consumed(oc, msp)
    def u_after_load(oc):
        if oc == 2:
            for i in range(4):
                dma('gpsimd', sPWO, wo_c[i * 512:(i + 1) * 512, :], wo[i * 512:(i + 1) * 512, :])
    projfm(wu, lambda k, tg: A2[:, k * 2048 + tg * TG: k * 2048 + (tg + 1) * TG],
           None, u_evac, "u", after_load=u_after_load)
    wait('gpsimd', sPWO, sPWO.n)
    PREWO = notify('gpsimd', nG)
    if dbg:
        for c in range(CH):
            wait_mark('sync', [u_sp_marks[c]])
            dma('sync', sDBG, dbg_t["du"][c * 128:(c + 1) * 128, :], u_sp[c * 128:(c + 1) * 128, :])
    wait('gpsimd', sAR, 1)
    wait_mark('gpsimd', wslot_ring.marks[0] + wslot_ring.marks[1])
    dma('gpsimd', sG, kv_bf[:, :], cc_out[:, :])
    wait('gpsimd', sG, sG.n)
    GKV = notify('gpsimd', nG)
    sPW0, sPW1 = Sem(nc, es, "sPW0"), Sem(nc, es, "sPW1")
    def _prestage_items():
        for i in range(32):
            yield (sPW0, w0c[i * 64:(i + 1) * 64, :], w0t[i * 64:(i + 1) * 64, :])
        for i in range(32):
            yield (sPW1, w1c[i * 256:(i + 1) * 256, :], w1t[i * 256:(i + 1) * 256, :])
    _pre_gen = _prestage_items()
    _pre_hist = []
    def emit_prestage_chunk(k):
        for _ in range(k):
            item = next(_pre_gen, None)
            if item is None:
                return
            sem, dst, src = item
            if len(_pre_hist) >= 3:
                wait('gpsimd', *_pre_hist[-3])
            _pre_hist.append(dma('gpsimd', sem, dst, src))
    if dbg:
        wait_mark('sync', [GKV])
        dma('sync', sDBG, dbg_t["dkv"][:, :], cc_out[:, :])

    snap("p8")
    qst4 = [qstr[0], qstr[1], urng[0], urng[1]]
    qst_ring = Ring(4)
    qst_ring.marks[0] = [KV_ALL]
    qst_ring.marks[1] = [KV_ALL]
    qst_ring.marks[2] = [u_sp_marks[14], KV_ALL]
    qst_ring.marks[3] = [u_sp_marks[15], KV_ALL]
    out_done = {}
    for h in range(NH):
        qslot = qst4[h % 4]
        wait_mark('sync', qst_ring.prod_wait(h))
        sql = slot_sem("QSL", h % 4)
        mq = dma('sync', sql, qslot[:, :], q_fm_sp[h * 128:(h + 1) * 128, :])
        wait_mark('tensor', [mq])
        if h == 0:
            wait_mark('tensor', [GKV])
        for tg in range(TGN):
            p = PROT[(h * TGN + tg) % 4]
            use_bank(p)
            mmm = sig('tensor', lambda t, p=p, h=h, tg=tg, qslot=qslot: t.matmul(
                p[:, :], kv_bf[:, h * 128:(h + 1) * 128],
                qslot[:, tg * TG:(tg + 1) * TG], start=True, stop=True))
            dst = A2[:, h * 2048 + tg * TG: h * 2048 + (tg + 1) * TG]
            wait_mark('scalar', [mmm])
            mev = sig('scalar', lambda s, p=p, dst=dst: s.activation(dst, p[:, :], AF.Copy))
            evM[id(p)].append(mev)
            out_done[(h, tg)] = mev
        qst_ring.consumed(h, out_done[(h, TGN - 1)])
    OUT_ALL = out_done[(NH - 1, TGN - 1)]
    if dbg:
        for c in range(CH):
            wait_mark('sync', [OUT_ALL])
            dma('sync', sDBG, dbg_t["dout"][c * 128:(c + 1) * 128, :], A2[:, c * 2048:(c + 1) * 2048])

    snap("p9")
    def osrc(c, tg):
        return A2[:, c * 2048 + tg * TG: c * 2048 + (tg + 1) * TG]
    ROWS_B = stats(osrc, range(TGN), TG, r_bf, s_bf, src_ready=lambda c: OUT_ALL)
    wait_mark('gpsimd', [OUT_ALL])
    emit_prestage_chunk(200)
    wait('sync', slot_sem("SPU", 0), slot_sem("SPU", 0).n)
    wait('sync', slot_sem("SPU", 1), slot_sem("SPU", 1).n)
    u_ld_ring = Ring(2)
    u_ld_ring.marks[0] = [u_sp_marks[14], out_done[(14, TGN - 1)]]
    u_ld_ring.marks[1] = [u_sp_marks[15], out_done[(15, TGN - 1)]]
    rB_b = at("rB_b", [128, 2048], BF16, 163840)
    sB_b = at("sB_b", [128, 2048], BF16, 139264)
    for tg in range(TGN):
        mb = bcast(PS1, r_bf[0:1, tg * TG:(tg + 1) * TG], TG,
                   extra=ROWS_B if tg == 0 else None)
        wait_mark('vector', [mb])
        if tg == 0:
            wait_mark('vector', [OUT_ALL])
        er = sig('vector', lambda v, tg=tg: v.tensor_copy(rB_b[:, tg * TG:(tg + 1) * TG], PS1[:, :TG]))
        evM[id(PS1)].append(er)
        mb2 = bcast(PS2, s_bf[0:1, tg * TG:(tg + 1) * TG], TG)
        wait_mark('vector', [mb2])
        esv = sig('vector', lambda v, tg=tg: v.tensor_copy(sB_b[:, tg * TG:(tg + 1) * TG], PS2[:, :TG]))
        evM[id(PS2)].append(esv)
    z_done = {}
    for oc in range(NH):
        uslot = urng[oc % 2]
        wait_mark('sync', u_ld_ring.prod_wait(oc))
        sul = slot_sem("UL", oc % 2)
        mu = dma('sync', sul, uslot[:, :], u_sp[oc * 128:(oc + 1) * 128, :])
        gcol = vsb[:, VOFF['g'] + oc:VOFF['g'] + oc + 1]
        bcol = vsb[:, VOFF['b'] + oc:VOFF['b'] + oc + 1]
        for tg in range(TGN):
            t1 = scrA[tg % 2]; t2 = scrB[tg % 2]
            src = A2[:, oc * 2048 + tg * TG: oc * 2048 + (tg + 1) * TG]
            on('vector', lambda v, src=src, t1=t1, tg=tg: v.tensor_tensor(
                t1[:, :], src, rB_b[:, tg * TG:(tg + 1) * TG], ALU.mult))
            on('vector', lambda v, t1=t1, t2=t2, tg=tg: v.tensor_tensor(
                t2[:, :], t1[:, :], sB_b[:, tg * TG:(tg + 1) * TG], ALU.add))
            on('vector', lambda v, t2=t2, t1=t1, gcol=gcol, bcol=bcol: v.tensor_scalar(
                t1[:, :], t2[:, :], gcol, bcol, ALU.mult, ALU.add))
            wait_mark('vector', [mu])
            mz = sig('vector', lambda v, t1=t1, uslot=uslot, tg=tg, src=src: v.tensor_tensor(
                src, t1[:, :], uslot[:, tg * TG:(tg + 1) * TG], ALU.mult))
            z_done[(oc, tg)] = mz
        u_ld_ring.consumed(oc, z_done[(oc, TGN - 1)])
    Z_ALL = z_done[(NH - 1, TGN - 1)]
    if dbg:
        for c in range(CH):
            wait_mark('sync', [Z_ALL])
            dma('sync', sDBG, dbg_t["dz"][c * 128:(c + 1) * 128, :], A2[:, c * 2048:(c + 1) * 2048])

    snap("p10")
    wo_slots = [wrng[0], wrng[1], w0rng[2], w0rng[3]]
    wo_ring = Ring(4)
    wo_ring.marks[0] = list(wslot_ring.marks[0]) + [OUT_ALL]
    wo_ring.marks[1] = list(wslot_ring.marks[1]) + [OUT_ALL]
    wo_ring.marks[2] = [OUT_ALL]
    wo_ring.marks[3] = [OUT_ALL]
    wo_it = [0]
    a_done = {}
    def a_evac(oc, tg, p):
        mmn = mmM[id(p)]
        bo = vsb[:, VOFF['bo'] + oc:VOFF['bo'] + oc + 1]
        dst = A1[:, oc * 2048 + tg * TG: oc * 2048 + (tg + 1) * TG]
        wait_mark('scalar', [mmn])
        mact = sig('scalar', lambda s, p=p, dst=dst, bo=bo: s.activation(dst, p[:, :], AF.Identity, bias=bo))
        evM[id(p)].append(mact)
        a_done[(oc, tg)] = mact
    projfm(wo_c, lambda k, tg: A2[:, k * 2048 + tg * TG: k * 2048 + (tg + 1) * TG],
           Z_ALL, a_evac, "wo", slots=wo_slots, ring=wo_ring, itc=wo_it,
           ld_eng='sync', ld_sem='WLY', pre_wait=PREWO)
    wait('gpsimd', sPW0, sPW0.n)
    PRE0 = notify('gpsimd', nG)
    wait('gpsimd', sPW1, sPW1.n)
    PRE1 = notify('gpsimd', nG)
    A_ALL = a_done[(CH - 1, TGN - 1)]
    if dbg:
        for c in range(CH):
            wait_mark('sync', [A_ALL])
            dma('sync', sDBG, dbg_t["da"][c * 128:(c + 1) * 128, :], A1[:, c * 2048:(c + 1) * 2048])

    snap("p11")
    def asrc(c, tg):
        return A1[:, c * 2048 + tg * TG: c * 2048 + (tg + 1) * TG]
    ROWS_C = stats(asrc, range(TGN), TG, r_bf, s_bf, src_ready=lambda c: A_ALL)
    sBt = at("sBt", [128, 2048], BF16, 176128)
    sb_marks = []
    for tg in range(TGN):
        mb = bcast(PS1, r_bf[0:1, tg * TG:(tg + 1) * TG], TG,
                   extra=ROWS_C if tg == 0 else None)
        wait_mark('vector', [mb])
        if tg == 0:
            wait_mark('vector', [Z_ALL])
        e1 = sig('vector', lambda v, tg=tg: v.tensor_copy(rBt[:, tg * TG:(tg + 1) * TG], PS1[:, :TG]))
        evM[id(PS1)].append(e1)
        mb2 = bcast(PS2, s_bf[0:1, tg * TG:(tg + 1) * TG], TG)
        wait_mark('vector', [mb2])
        e2s = sig('vector', lambda v, tg=tg: v.tensor_copy(sBt[:, tg * TG:(tg + 1) * TG], PS2[:, :TG]))
        evM[id(PS2)].append(e2s)
        sb_marks.append(e2s)
    x_ring = Ring(2)
    y1_ring = Ring(2)
    last_y1w = None
    y1_vdone = None
    y1c_marks = []
    for c in range(CH):
        for half in range(2):
            i2 = c * 2 + half
            wait_mark('sync', x_ring.prod_wait(i2))
            sxr = slot_sem("XR", i2 % 2)
            mx = dma('sync', sxr, xrng[i2 % 2][:, :],
                     xT[c * 128:(c + 1) * 128, half * 1024:(half + 1) * 1024])
            if i2 == 0:
                wait_mark('vector', sb_marks)
                wait_mark('scalar', [A_ALL])
            on('vector', lambda v, c=c, half=half: v.tensor_tensor(
                ctscr[:, half * 1024:(half + 1) * 1024],
                A1[:, c * 2048 + half * 1024: c * 2048 + (half + 1) * 1024],
                rBt[:, half * 1024:(half + 1) * 1024], ALU.mult))
            on('vector', lambda v, half=half: v.tensor_tensor(
                ctscr[:, half * 1024:(half + 1) * 1024], ctscr[:, half * 1024:(half + 1) * 1024],
                sBt[:, half * 1024:(half + 1) * 1024], ALU.add))
            wait_mark('vector', [mx])
            wait_mark('vector', y1_ring.prod_wait(i2))
            yslot = y1rng[i2 % 2]
            my1 = sig('vector', lambda v, half=half, yslot=yslot, i2=i2: v.tensor_tensor(
                yslot[:, :], ctscr[:, half * 1024:(half + 1) * 1024],
                xrng[i2 % 2][:, :], ALU.add))
            x_ring.consumed(i2, my1)
            wait_mark('scalar', [my1])
            mcp = sig('scalar', lambda sc, yslot=yslot, c=c, half=half: sc.activation(
                A2[:, c * 2048 + half * 1024: c * 2048 + (half + 1) * 1024], yslot[:, :], AF.Copy))
            if half == 1:
                y1c_marks.append(mcp)
            y1_vdone = mcp
            wait_mark('sync', [my1])
            syw = slot_sem("Y1W", i2 % 2)
            mw = dma('sync', syw, y1_sp[c * 128:(c + 1) * 128, half * 1024:(half + 1) * 1024],
                     yslot[:, :])
            y1_ring.consumed(i2, mw, mcp)
            last_y1w = mw
    Y1_V = y1_vdone
    if dbg:
        wait('sync', slot_sem("Y1W", 0), slot_sem("Y1W", 0).n)
        wait('sync', slot_sem("Y1W", 1), slot_sem("Y1W", 1).n)
        for c in range(CH):
            dma('sync', sDBG, dbg_t["dy1"][c * 128:(c + 1) * 128, :], y1_sp[c * 128:(c + 1) * 128, :])

    snap("p12")
    def y1src(c, tg):
        return A2[:, c * 2048 + tg * TG: c * 2048 + (tg + 1) * TG]
    ROWS_D = stats(y1src, range(TGN), TG, rd_bf, sd_bf, src_ready=lambda c: y1c_marks[c])

    snap("p13")
    wait('sync', slot_sem("Y1W", 0), slot_sem("Y1W", 0).n)
    wait('sync', slot_sem("Y1W", 1), slot_sem("Y1W", 1).n)
    YW_ALL = notify('sync', nY)
    w0_ring = Ring(4)
    for _j in range(4):
        w0_ring.marks[_j] = list(wo_ring.marks[_j]) + [Y1_V]
    w1_ring = Ring(2)
    w1_ring.seed_all(Y1_V, KV_ALL)
    fy1_ring = Ring(2)
    fy2_ring = Ring(2)
    fy2_ring.seed_all(last_y1w)
    fy1_it = [0]; fy2_it = [0]
    w0_it = [0]; w1_it = [0]
    last_out = None
    def x2_emit(bb, prior_ffn1_mark):
        tok0b = bb * FT
        use_bank(PD)
        m1 = sig('tensor', lambda t, tok0b=tok0b: t.matmul(
            PD[:, :FT], ones_row[0:1, 0:128], rd_bf[0:1, tok0b:tok0b + FT], start=True, stop=True))
        wait_mark('vector', [m1])
        e1 = sig('vector', lambda v: v.tensor_copy(rdBb[:, :], PD[:, :FT]))
        evM[id(PD)].append(e1)
        use_bank(PD)
        m2 = sig('tensor', lambda t, tok0b=tok0b: t.matmul(
            PD[:, :FT], ones_row[0:1, 0:128], sd_bf[0:1, tok0b:tok0b + FT], start=True, stop=True))
        wait_mark('vector', [m2])
        e2 = sig('vector', lambda v: v.tensor_copy(sdBb[:, :], PD[:, :FT]))
        evM[id(PD)].append(e2)
        mx2 = None
        for c in range(CH):
            it = fy1_it[0]; fy1_it[0] += 1
            wait_mark('sync', fy1_ring.prod_wait(it))
            sf = slot_sem("FY1", it % 2)
            my = dma('sync', sf, fy1rng[it % 2][:, :],
                     y1_sp[c * 128:(c + 1) * 128, tok0b:tok0b + FT])
            wait_mark('vector', [my])
            if c == 0 and prior_ffn1_mark is not None:
                wait_mark('vector', [prior_ffn1_mark])
            t1 = scrA[c % 2]
            on('vector', lambda v, it=it, t1=t1: v.tensor_tensor(t1[:, :FT], fy1rng[it % 2][:, :], rdBb[:, :], ALU.mult))
            mx2 = sig('vector', lambda v, c=c, t1=t1: v.tensor_tensor(
                A2[:, c * 512:(c + 1) * 512], t1[:, :FT], sdBb[:, :], ALU.add))
            fy1_ring.consumed(it, mx2)
        return mx2

    w0_loads = {}
    w0_ptr = [0]
    def w0_emit_load():
        idx = w0_ptr[0]
        if idx >= FB * NFC:
            return
        w0_ptr[0] += 1
        ncn = idx % NFC
        j = idx % 4
        slot = w0rng[j]
        if idx == 0:
            wait_mark('sync', [PRE0])
        wait_mark('sync', w0_ring.prod_wait(idx))
        mld = dma('sync', slot_sem("WLY", j),
                  slot[:, :].rearrange("p (c n) -> p c n", n=128),
                  w0c[:, ncn * 128:(ncn + 1) * 128].rearrange("(c p) n -> p c n", p=128))
        w0_loads[idx] = (mld, slot)

    X2_MARKS = {}
    for b in range(FB):
        tok0 = b * FT
        if b == 0:
            wait_mark('tensor', [ROWS_D])
            X2_MARKS[0] = x2_emit(0, None)
        X2_ALL = X2_MARKS[b]
        if dbg and b == 0:
            wait_mark('sync', [X2_ALL])
            dma('sync', sDBG, dbg_t["dx2"][:, :], A2[:, 0:8192])

        for ncn in range(NFC):
            idx = b * NFC + ncn
            while w0_ptr[0] <= min(idx + 3, FB * NFC - 1):
                w0_emit_load()
            mld, slot = w0_loads[idx]
            p = PROT[ncn % 4]
            use_bank(p)
            wait_mark('tensor', [mld])
            if ncn == 0:
                wait_mark('tensor', [X2_ALL])
            for k in range(CH - 1):
                on('tensor', lambda t, p=p, slot=slot, k=k: t.matmul(
                    p[:, :], slot[:, k * 128:(k + 1) * 128],
                    A2[:, k * 512:(k + 1) * 512], start=(k == 0), stop=False))
            mmm = sig('tensor', lambda t, p=p, slot=slot: t.matmul(
                p[:, :], slot[:, (CH - 1) * 128:CH * 128],
                A2[:, (CH - 1) * 512:CH * 512], start=False, stop=True))
            w0_ring.consumed(idx, mmm)
            if idx + 4 <= FB * NFC - 1 and w0_ptr[0] == idx + 4:
                w0_emit_load()
            if ncn == NFC - 1:
                ffn1_last = mmm
            b0c = vsb[:, VOFF['b0'] + ncn:VOFF['b0'] + ncn + 1]
            wait_mark('scalar', [mmm])
            if ncn == 0:
                wait_mark('scalar', [X2_ALL])
            mev = sig('scalar', lambda s, p=p, ncn=ncn, b0c=b0c: s.activation(
                A1[:, ncn * 512:(ncn + 1) * 512], p[:, :], AF.Gelu, bias=b0c))
            evM[id(p)].append(mev)
            if ncn == NFC - 1:
                D_ALL = mev
        if b == 0:
            if dbg:
                wait_mark('sync', [D_ALL])
                dma('sync', sDBG, dbg_t["dD"][:, :], A1[:, 0:32768])

        if b + 1 < FB:
            X2_MARKS[b + 1] = x2_emit(b + 1, ffn1_last)
        lne_pend = []
        lne_mm_last = [None]
        def _flush_lne_stats():
            if not lne_pend:
                return
            hc0, sqslot, msq, it_sq = lne_pend.pop(0)
            wait_mark('tensor', [msq])
            on('tensor', lambda t, hc0=hc0: t.matmul(
                PS1[0:1, :FT], onesc[:, :], A2[:, 8192 + hc0 * 512: 8192 + (hc0 + 1) * 512],
                start=(hc0 == 0), stop=(hc0 == CH - 1)))
            mc = sig('tensor', lambda t, sqslot=sqslot, hc0=hc0: t.matmul(
                PS2[0:1, :FT], onesc[:, :], sqslot[:, :FT],
                start=(hc0 == 0), stop=(hc0 == CH - 1)))
            sq_ring.consumed(it_sq, mc)
            lne_mm_last[0] = mc
        for hc in range(CH):
            def w1_load(pc, hc=hc):
                it = w1_it[0]; w1_it[0] += 1
                j = it % 2
                off = j * 2048
                if b == 0 and hc == 0 and pc == 0:
                    wait_mark('sync', [PRE1])
                wait_mark('sync', w1_ring.prod_wait(it))
                mld = dma('sync', slot_sem("W1LY", j),
                          w1rng[:, off:off + 2048].rearrange("p (m h) -> p m h", h=128),
                          w1t[pc * 16 * 128:(pc + 1) * 16 * 128, hc * 128:(hc + 1) * 128]
                          .rearrange("(m p) h -> p m h", p=128) if False else
                          w1c[pc * 16 * 128:(pc + 1) * 16 * 128, hc * 128:(hc + 1) * 128]
                          .rearrange("(m p) h -> p m h", p=128))
                return (mld, off, it, None)
            pending = [w1_load(0), w1_load(1)]
            p = PROT[hc % 3]
            use_bank(p)
            if hc == 0:
                wait_mark('tensor', [D_ALL])
            mmm = None
            for pc in range(4):
                mld, off, it, mwb = pending[pc]
                wait_mark('tensor', [mld])
                for j2 in range(15):
                    ncn = pc * 16 + j2
                    on('tensor', lambda t, p=p, off=off, j2=j2, ncn=ncn: t.matmul(
                        p[:, :], w1rng[:, off + j2 * 128: off + (j2 + 1) * 128],
                        A1[:, ncn * 512:(ncn + 1) * 512],
                        start=(ncn == 0), stop=False))
                ncn = pc * 16 + 15
                mmm = sig('tensor', lambda t, p=p, off=off, ncn=ncn: t.matmul(
                    p[:, :], w1rng[:, off + 15 * 128: off + 16 * 128],
                    A1[:, ncn * 512:(ncn + 1) * 512],
                    start=False, stop=(ncn == NFC - 1)))
                w1_ring.consumed(it, mmm, mwb)
                if pc + 2 < 4:
                    pending.append(w1_load(pc + 2))
            b1c = vsb[:, VOFF['b1'] + hc:VOFF['b1'] + hc + 1]
            wait_mark('scalar', [mmm])
            if hc == 0:
                wait_mark('scalar', [X2_ALL])
            mev = sig('scalar', lambda s, p=p, hc=hc, b1c=b1c: s.activation(
                A2[:, 8192 + hc * 512: 8192 + (hc + 1) * 512], p[:, :], AF.Identity, bias=b1c))
            evM[id(p)].append(mev)
            if hc == 0:
                use_bank(PS1); use_bank(PS2)
            it_sq = sq_it[0]; sq_it[0] += 1
            sqslot = SQ4[it_sq % 4]
            wait_mark('scalar', sq_ring.prod_wait(it_sq))
            msq = sig('scalar', lambda s, sqslot=sqslot, hc=hc: s.square(
                sqslot[:, :FT], A2[:, 8192 + hc * 512: 8192 + (hc + 1) * 512]))
            lne_pend.append((hc, sqslot, msq, it_sq))
            if hc >= 1:
                _flush_lne_stats()
            if hc == CH - 1:
                E_ALL = mev
        _flush_lne_stats()
        _flush_lne_stats()
        if b == 0:
            if dbg:
                wait_mark('sync', [E_ALL])
                dma('sync', sDBG, dbg_t["dE"][:, :], A2[:, 8192:16384])

        mm_e = lne_mm_last[0]
        wait_mark('vector', [mm_e])
        e1 = sig('vector', lambda v: v.tensor_copy(s1t[:, :FT], PS1[0:1, :FT]))
        e2 = sig('vector', lambda v: v.tensor_copy(s2t[:, :FT], PS2[0:1, :FT]))
        evM[id(PS1)].append(e1)
        evM[id(PS2)].append(e2)
        on('vector', lambda v: v.tensor_scalar(s1t[:, :FT], s1t[:, :FT], 1.0 / D, None, ALU.mult))
        on('vector', lambda v: v.tensor_tensor(tmpt[:, :FT], s1t[:, :FT], s1t[:, :FT], ALU.mult))
        on('vector', lambda v: v.tensor_scalar(s2t[:, :FT], s2t[:, :FT], 1.0 / D, EPS, ALU.mult, ALU.add))
        on('vector', lambda v: v.tensor_tensor(s2t[:, :FT], s2t[:, :FT], tmpt[:, :FT], ALU.subtract))
        mrec = sig('vector', lambda v: v.reciprocal(s2t[:, :FT], s2t[:, :FT]))
        wait_mark('scalar', [mrec])
        mr = sig('scalar', lambda s: s.activation(r_bf[0:1, :FT], s2t[:, :FT], AF.Sqrt))
        wait_mark('vector', [mr])
        ROWS_E = sig('vector', lambda v: v.scalar_tensor_tensor(
            s_bf[0:1, :FT], s1t[:, :FT], -1.0, r_bf[0:1, :FT], ALU.mult, ALU.mult))
        mbr = bcast(PS1, r_bf[0:1, :FT], FT, extra=ROWS_E)
        mbs = bcast(PS2, s_bf[0:1, :FT], FT)
        ln_e_last = None
        for c in range(CH):
            it = fy1_it[0]; fy1_it[0] += 1
            wait_mark('sync', fy1_ring.prod_wait(it))
            sf = slot_sem("FY1", it % 2)
            my = dma('sync', sf, fy1rng[it % 2][:, :],
                     y1_sp[c * 128:(c + 1) * 128, tok0:tok0 + FT])
            if c == 0:
                wait_mark('vector', [mbr, mbs])
            t1 = scrA[c % 2]
            on('vector', lambda v, c=c, t1=t1: v.tensor_tensor(
                t1[:, :FT], A2[:, 8192 + c * 512: 8192 + (c + 1) * 512], PS1[:, :FT], ALU.mult))
            on('vector', lambda v, t1=t1: v.tensor_tensor(t1[:, :FT], t1[:, :FT], PS2[:, :FT], ALU.add))
            wait_mark('vector', [my])
            it2 = fy2_it[0]; fy2_it[0] += 1
            wait_mark('vector', fy2_ring.prod_wait(it2))
            yslot = fy2rng[it2 % 2]
            my2 = sig('vector', lambda v, t1=t1, it=it, yslot=yslot: v.tensor_tensor(
                yslot[:, :], t1[:, :FT], fy1rng[it % 2][:, :], ALU.add))
            fy1_ring.consumed(it, my2)
            ln_e_last = my2
            wait_mark('sync', [my2])
            so = slot_sem("OUT", it2 % 2)
            mo = dma('sync', so, yT[c * 128:(c + 1) * 128, tok0:tok0 + FT], yslot[:, :])
            fy2_ring.consumed(it2, mo)
            last_out = mo
        evM[id(PS1)].append(ln_e_last)
        evM[id(PS2)].append(ln_e_last)

    wait('sync', slot_sem("OUT", 0), slot_sem("OUT", 0).n)
    wait('sync', slot_sem("OUT", 1), slot_sem("OUT", 1).n)
    if dbg:
        wait('sync', sDBG, sDBG.n)
    snap("all")

    if stop_after is not None:
        cut = phase_snaps[stop_after]
        for e in q:
            q[e] = q[e][:cut[e]]

    block = es.enter_context(nc.Block())
    with es:
        for eng in ("tensor", "vector", "scalar", "gpsimd", "sync"):
            fns = q[eng]
            def body(e, fns=fns):
                for fn in fns:
                    fn(e)
            getattr(block, eng)(body)
    return nc


def make_in_maps(ins):
    x = np.asarray(ins['L0_attn_x'])[0]
    xT_full = np.ascontiguousarray(x.T)
    vecs = pack_vecs(ins)
    grow = np.stack([np.asarray(ins['L0_attn_ln_g'], np.float32),
                     np.asarray(ins['L0_attn_ln_b'], np.float32)])
    w0tA = np.ascontiguousarray(np.asarray(ins['L0_ffn_W0'], np.float32).T)
    w1tA = np.ascontiguousarray(np.asarray(ins['L0_ffn_W1'], np.float32).T)
    idnt = np.eye(128, dtype=np.float32)
    onesr = np.ones((1, TG), np.float32)
    in_maps = []
    for c in range(NCORES):
        in_maps.append({
            "xT": np.ascontiguousarray(xT_full[:, c * T:(c + 1) * T]),
            "wqk": np.asarray(ins['L0_attn_Wqk'], np.float32),
            "wv": np.asarray(ins['L0_attn_Wv'], np.float32),
            "wu": np.asarray(ins['L0_attn_Wu'], np.float32),
            "wo": np.asarray(ins['L0_attn_Wo'], np.float32),
            "w0t": w0tA, "w1t": w1tA, "vecs": vecs, "grow": grow,
            "idnt": idnt, "onesr": onesr,
        })
    return in_maps


def kernel(**ins):
    nc = build_nc(dbg=False)
    in_maps = make_in_maps(ins)
    res = run_bass_kernel_spmd(nc, in_maps, core_ids=list(range(NCORES)))
    out = np.concatenate([res.results[c]["yT"] for c in range(NCORES)], axis=1)
    return np.ascontiguousarray(out.T)[None].astype(np.float32)


# version 14 (final)
# speedup vs baseline: 1.0491x; 1.0491x over previous
import numpy as np
import ml_dtypes
from contextlib import ExitStack
import concourse.bass as bass
import concourse.mybir as mybir
from concourse.bass_utils import run_bass_kernel_spmd

F32 = mybir.dt.float32
BF16 = mybir.dt.bfloat16
AF = mybir.ActivationFunctionType
ALU = mybir.AluOpType

NCORES = 8
D, T, CH, TGN, TG, TTN = 2048, 2048, 16, 4, 512, 16
NF, NFC, NH, FB, FT = 8192, 64, 16, 4, 512
EPS = 1e-5
VOFF = {'bqk': 0, 'bv': 16, 'bu': 32, 'bo': 48, 'g': 64, 'b': 80, 'b1': 96, 'b0': 112}
NV = 176

def pack_vecs(ins):
    v = np.zeros((128, NV), np.float32)
    def put(name, arr):
        a = np.asarray(arr, np.float32).reshape(-1, 128).T
        v[:, VOFF[name]:VOFF[name] + a.shape[1]] = a
    put('bqk', ins['L0_attn_bqk']); put('bv', ins['L0_attn_bv'])
    put('bu', ins['L0_attn_bu']); put('bo', ins['L0_attn_bo'])
    put('g', ins['L0_attn_ln_g']); put('b', ins['L0_attn_ln_b'])
    put('b1', ins['L0_ffn_bias1']); put('b0', ins['L0_ffn_bias0'])
    return v

class Sem:
    def __init__(self, nc, es, name):
        self.h = es.enter_context(nc.semaphore(name))
        self.n = 0
        self.name = name
    def bump(self, k=1):
        self.n += k
        return self.n

class Ring:
    def __init__(self, nslots):
        self.marks = [[] for _ in range(nslots)]
        self.ns = nslots
    def prod_wait(self, i):
        return list(self.marks[i % self.ns])
    def consumed(self, i, *marks):
        self.marks[i % self.ns] = [m for m in marks if m is not None]
    def seed_all(self, *marks):
        for i in range(self.ns):
            self.marks[i] = [m for m in marks if m is not None]

def build_nc(dbg=False, stop_after=None, simple_elu=False, elu_mode=4, dbgb=0):
    nc = bass.Bass("TRN2", target_bir_lowering=False, detect_race_conditions=False)
    es = ExitStack()

    xT   = nc.dram_tensor("xT",   [D, T], F32, kind="ExternalInput").ap()
    wqk  = nc.dram_tensor("wqk",  [D, D], F32, kind="ExternalInput").ap()
    wv   = nc.dram_tensor("wv",   [D, D], F32, kind="ExternalInput").ap()
    wu   = nc.dram_tensor("wu",   [D, D], F32, kind="ExternalInput").ap()
    wo   = nc.dram_tensor("wo",   [D, D], F32, kind="ExternalInput").ap()
    w0t  = nc.dram_tensor("w0t",  [D, NF], F32, kind="ExternalInput").ap()
    w1t  = nc.dram_tensor("w1t",  [NF, D], F32, kind="ExternalInput").ap()
    vecs = nc.dram_tensor("vecs", [128, NV], F32, kind="ExternalInput").ap()
    grow = nc.dram_tensor("grow", [2, D], F32, kind="ExternalInput").ap()
    idnt = nc.dram_tensor("idnt", [128, 128], F32, kind="ExternalInput").ap()
    onesr = nc.dram_tensor("onesr", [1, TG], F32, kind="ExternalInput").ap()
    yT   = nc.dram_tensor("yT",   [D, T], F32, kind="ExternalOutput").ap()

    q_fm_sp = nc.dram_tensor("q_fm_sp", [D, T], BF16, kind="Internal").ap()
    q_tm_sp = nc.dram_tensor("q_tm_sp", [D, T], BF16, kind="Internal").ap()
    u_sp    = nc.dram_tensor("u_sp",    [D, T], BF16, kind="Internal").ap()
    y1_sp   = nc.dram_tensor("y1_sp",   [D, T], F32, kind="Internal").ap()
    w0c     = nc.dram_tensor("w0c", [D, NF], BF16, kind="Internal").ap()
    wo_c    = nc.dram_tensor("wo_c", [D, D], BF16, kind="Internal").ap()
    w1c     = nc.dram_tensor("w1c", [NF, D], BF16, kind="Internal").ap()
    cc_in   = nc.dram_tensor("cc_in",  [128, NH * 128], F32, kind="Internal").ap()
    cc_out  = nc.dram_tensor("cc_out", [128, NH * 128], F32, kind="Internal", addr_space="Shared").ap()

    dbg_t = {}
    if dbg:
        for nm in ["dx1", "dq", "dv", "du", "dout", "dz", "da"]:
            dbg_t[nm] = nc.dram_tensor(nm, [D, T], BF16, kind="ExternalOutput").ap()
        dbg_t["dy1"] = nc.dram_tensor("dy1", [D, T], F32, kind="ExternalOutput").ap()
        dbg_t["dx2"] = nc.dram_tensor("dx2", [128, 8192], BF16, kind="ExternalOutput").ap()
        dbg_t["dD"] = nc.dram_tensor("dD", [128, 32768], BF16, kind="ExternalOutput").ap()
        dbg_t["dE"] = nc.dram_tensor("dE", [128, 8192], BF16, kind="ExternalOutput").ap()
        dbg_t["dkv"] = nc.dram_tensor("dkv", [128, NH * 128], F32, kind="ExternalOutput").ap()

    MB = 24576
    def at(name, shape, dt, off):
        return nc.alloc_sbuf_tensor_at(name, shape, dt, offset=MB + off).ap()

    A1   = at("A1", [128, 32768], BF16, 0)
    A2   = at("A2", [128, 32768], BF16, 65536)
    xrng = [at(f"xr{i}", [128, 1024], F32, 131072 + i * 4096) for i in range(2)]
    kvacc = at("kvacc", [128, 2048], F32, 139264)
    y1rng = [at(f"y1r{i}", [128, 1024], F32, 139264 + i * 4096) for i in range(2)]
    escr  = [at(f"escr{i}", [128, 512], F32, 139264 + i * 2048) for i in range(2)]
    fy1rng = [at(f"fy1r{i}", [128, 512], F32, 139264 + i * 2048) for i in range(2)]
    fy2rng = [at(f"fy2r{i}", [128, 512], F32, 143360 + i * 2048) for i in range(2)]
    g_row = at("g_row", [1, D], BF16, 147456)
    b_row = at("b_row", [1, D], BF16, 151552)
    rd_bf = at("rd_bf", [1, D], BF16, 131072)
    sd_bf = at("sd_bf", [1, D], BF16, 135168)
    wrng  = [at(f"wr{i}", [128, 2048], BF16, 155648 + i * 4096) for i in range(2)]
    qtmr  = [at(f"qtm{i}", [128, 2048], BF16, 163840 + i * 4096) for i in range(2)]
    kv_bf = at("kv_bf", [128, 2048], BF16, 155648)
    qstr  = [at(f"qst{i}", [128, 2048], BF16, 159744 + i * 4096) for i in range(2)]
    w0rng = [at(f"w0r{i}", [128, 2048], BF16, 155648 + i * 4096) for i in range(4)]
    w1rng = at("w1r", [128, 4096], BF16, 172032)
    ctscr = at("ctscr", [128, 2048], BF16, 163840)
    urng  = [at(f"ur{i}", [128, 2048], BF16, 172032 + i * 4096) for i in range(2)]
    rBt   = at("rBt", [128, 2048], BF16, 172032)
    vtmr  = [at(f"vtm{i}", [128, 2048], BF16, 172032 + i * 4096) for i in range(2)]
    sqr   = [at(f"sq{i}", [128, 512], BF16, 180224 + i * 1024) for i in range(2)]
    s1t  = at("s1t", [1, 512], F32, 182272)
    s2t  = at("s2t", [1, 512], F32, 184320)
    tmpt = at("tmpt", [1, 512], F32, 186368)
    r_bf = at("r_bf", [1, D], BF16, 188416)
    s_bf = at("s_bf", [1, D], BF16, 192512)
    vsb   = at("vsb", [128, NV], F32, 196608)
    nvsb  = at("nvsb", [128, 16], F32, 197312)
    idbf  = at("idbf", [128, 128], BF16, 197376)
    onesc = at("onesc", [128, 1], BF16, 197632)
    ones_row = at("ones_row", [1, TG], BF16, 197664)
    scrA  = [at(f"scrA{i}", [128, 512], BF16, 198688 + i * 1024) for i in range(2)]
    rdBb  = at("rdBb", [128, 512], BF16, 202784)
    sdBb  = at("sdBb", [128, 512], BF16, 147456)
    scrB  = [at(f"scrB{i}", [128, 512], BF16, 200736 + i * 1024) for i in range(2)]

    ps = lambda name, shape, dt=F32: es.enter_context(nc.psum_tensor(name, shape, dt))
    PA, PB, PC, PD = (ps(f"P{i}", [128, 512]) for i in "ABCD")
    PS1, PS2 = ps("PS1", [128, 512]), ps("PS2", [128, 512])
    PT1, PT2 = ps("PT1", [128, 1024], BF16), ps("PT2", [128, 1024], BF16)
    PROT = [PA, PB, PC, PD]

    S = lambda nm: Sem(nc, es, nm)
    cT, cV, cS = S("cT"), S("cV"), S("cS")
    CENG = {'tensor': cT, 'vector': cV, 'scalar': cS}
    nG, nY = S("nG"), S("nY")
    sAR = S("sAR")
    sG, sQFS, sDBG = S("sG"), S("sQFS"), S("sDBG")
    slot_sems = {}
    def slot_sem(name, j):
        if (name, j) not in slot_sems:
            slot_sems[(name, j)] = S(f"{name}{j}")
        return slot_sems[(name, j)]

    q = {e: [] for e in ("tensor", "vector", "scalar", "gpsimd", "sync")}
    phase_snaps = {}
    def snap(name):
        phase_snaps[name] = {e: len(q[e]) for e in q}
    def on(eng, fn):
        q[eng].append(fn)
    def wait(eng, sem, val):
        if val and val > 0:
            h = sem.h if isinstance(sem, Sem) else sem
            on(eng, lambda e, h=h, val=val: e.wait_ge(h, val))
    def wait_mark(eng, marks):
        for mk in (marks or []):
            if mk is not None:
                wait(eng, mk[0], mk[1])
    def sig(eng, fn):
        c = CENG[eng]
        n = c.bump()
        def w(e, fn=fn, c=c):
            fn(e).then_inc(c.h, 1)
        on(eng, w)
        return (c, n)
    def notify(eng, sem):
        n = sem.bump()
        on(eng, lambda e, sem=sem: e.nop().then_inc(sem.h, 1))
        return (sem, n)
    def dma(eng, sem, out_ap, in_ap):
        n = sem.bump(16)
        on(eng, lambda e, sem=sem, out_ap=out_ap, in_ap=in_ap:
            e.dma_start(out=out_ap, in_=in_ap).then_inc(sem.h, 16))
        return (sem, n)

    mmM = {}
    evM = {id(p): [] for p in (PA, PB, PC, PD, PS1, PS2, PT1, PT2)}
    def use_bank(p):
        wait_mark('tensor', evM[id(p)])
        evM[id(p)] = []

    SQ4 = [sqr[0], sqr[1], scrA[0], scrA[1]]
    for dst, src in [(vsb, vecs), (idbf, idnt), (g_row, grow[0:1, :]),
                     (b_row, grow[1:2, :]), (ones_row, onesr)]:
        dma('gpsimd', sG, dst, src)
    x_marks = []
    for c in range(CH):
        sx = slot_sem("XLD", c % 4)
        if c >= 4:
            wait('gpsimd', sx, sx.n)
        x_marks.append(dma('gpsimd', sx, A1[:, c * 2048:(c + 1) * 2048], xT[c * 128:(c + 1) * 128, :]))
    wait('gpsimd', sG, sG.n)
    for j in range(4):
        wait('gpsimd', slot_sem("XLD", j), slot_sem("XLD", j).n)
    GXRD = notify('gpsimd', nG)

    on('vector', lambda v: v.memset(onesc[:, :], 1.0))
    wait_mark('vector', [GXRD])
    CONSTS_RDY = sig('vector', lambda v: v.tensor_scalar(
        nvsb[:, :], vsb[:, VOFF['bqk']:VOFF['bqk'] + 16], -1.0, None, ALU.mult))

    snap("p1")
    sq_ring = Ring(4)
    sq_it = [0]
    def stats(src_ap_fn, tg_list, tgw, rdst, sdst, src_ready):
        rows_mark = None
        for tg in tg_list:
            use_bank(PS1); use_bank(PS2)
            for c in range(CH):
                it = sq_it[0]; sq_it[0] += 1
                slot = SQ4[it % 4]
                wait_mark('scalar', sq_ring.prod_wait(it))
                if src_ready:
                    wait_mark('scalar', [src_ready(c)])
                msq = sig('scalar', lambda s, slot=slot, ap=src_ap_fn(c, tg), tgw=tgw:
                          s.square(slot[:, :tgw], ap))
                if src_ready:
                    wait_mark('tensor', [src_ready(c)])
                on('tensor', lambda t, ap=src_ap_fn(c, tg), c=c, tgw=tgw: t.matmul(
                    PS1[0:1, :tgw], onesc[:, :], ap, start=(c == 0), stop=(c == CH - 1)))
                wait_mark('tensor', [msq])
                mc = sig('tensor', lambda t, slot=slot, c=c, tgw=tgw: t.matmul(
                    PS2[0:1, :tgw], onesc[:, :], slot[:, :tgw],
                    start=(c == 0), stop=(c == CH - 1)))
                sq_ring.consumed(it, mc)
                if c == CH - 1:
                    mm_stats = mc
            wait_mark('vector', [mm_stats])
            e1 = sig('vector', lambda v, tgw=tgw: v.tensor_copy(s1t[:, :tgw], PS1[0:1, :tgw]))
            e2 = sig('vector', lambda v, tgw=tgw: v.tensor_copy(s2t[:, :tgw], PS2[0:1, :tgw]))
            evM[id(PS1)].append(e1)
            evM[id(PS2)].append(e2)
            on('vector', lambda v, tgw=tgw: v.tensor_scalar(s1t[:, :tgw], s1t[:, :tgw], 1.0 / D, None, ALU.mult))
            on('vector', lambda v, tgw=tgw: v.tensor_tensor(tmpt[:, :tgw], s1t[:, :tgw], s1t[:, :tgw], ALU.mult))
            on('vector', lambda v, tgw=tgw: v.tensor_scalar(s2t[:, :tgw], s2t[:, :tgw], 1.0 / D, EPS, ALU.mult, ALU.add))
            on('vector', lambda v, tgw=tgw: v.tensor_tensor(s2t[:, :tgw], s2t[:, :tgw], tmpt[:, :tgw], ALU.subtract))
            mrec = sig('vector', lambda v, tgw=tgw: v.reciprocal(s2t[:, :tgw], s2t[:, :tgw]))
            wait_mark('scalar', [mrec])
            mr = sig('scalar', lambda s, tg=tg, tgw=tgw: s.activation(
                rdst[0:1, tg * tgw:(tg + 1) * tgw], s2t[:, :tgw], AF.Sqrt))
            wait_mark('vector', [mr])
            rows_mark = sig('vector', lambda v, tg=tg, tgw=tgw: v.scalar_tensor_tensor(
                sdst[0:1, tg * tgw:(tg + 1) * tgw], s1t[:, :tgw], -1.0,
                rdst[0:1, tg * tgw:(tg + 1) * tgw], ALU.mult, ALU.mult))
        return rows_mark

    def bcast(p, row_ap, n, extra=None):
        use_bank(p)
        wait_mark('tensor', [extra] if extra else [])
        m = sig('tensor', lambda t, p=p, row_ap=row_ap, n=n: t.matmul(
            p[:, :n], ones_row[0:1, 0:128], row_ap, start=True, stop=True))
        mmM[id(p)] = m
        return m

    wait_mark('tensor', [CONSTS_RDY])

    def xsrc(c, tg):
        return A1[:, c * 2048 + tg * TG: c * 2048 + (tg + 1) * TG]
    ROWS_A = stats(xsrc, range(TGN), TG, r_bf, s_bf, src_ready=lambda c: x_marks[c])
    wait_mark('vector', [GXRD])
    x1_done = {}
    for tg in range(TGN):
        mb1 = bcast(PS1, r_bf[0:1, tg * TG:(tg + 1) * TG], TG, extra=ROWS_A)
        mb2 = bcast(PS2, s_bf[0:1, tg * TG:(tg + 1) * TG], TG)
        wait_mark('vector', [mb1])
        e1 = sig('vector', lambda v, tg=tg: v.tensor_copy(rBt[:, tg * TG:(tg + 1) * TG], PS1[:, :TG]))
        evM[id(PS1)].append(e1)
        wait_mark('vector', [mb2])
        last_e = None
        for c in range(CH):
            src = A1[:, c * 2048 + tg * TG: c * 2048 + (tg + 1) * TG]
            dst = A2[:, c * 2048 + tg * TG: c * 2048 + (tg + 1) * TG]
            on('vector', lambda v, src=src, dst=dst, tg=tg: v.tensor_tensor(
                dst, src, rBt[:, tg * TG:(tg + 1) * TG], ALU.mult))
            m = sig('vector', lambda v, dst=dst: v.tensor_tensor(dst, dst, PS2[:, :TG], ALU.add))
            x1_done[(c, tg)] = m
            last_e = m
        evM[id(PS2)].append(last_e)
    X1_ALL = x1_done[(CH - 1, TGN - 1)]

    if dbg:
        for c in range(CH):
            wait_mark('sync', [X1_ALL])
            dma('sync', sDBG, dbg_t["dx1"][c * 128:(c + 1) * 128, :], A2[:, c * 2048:(c + 1) * 2048])

    snap("p3")
    wslot_ring = Ring(2)
    wslot_it = [0]
    def projfm(wdram, mov_ap, mov_wait, evac, prefix, slots=None, ring=None, itc=None,
               after_load=None, ld_eng='gpsimd', ld_sem='WL', pre_wait=None):
        slots = slots or wrng
        ring = ring if ring is not None else wslot_ring
        itc = itc if itc is not None else wslot_it
        nsl = len(slots)
        for oc in range(CH):
            it = itc[0]; itc[0] += 1
            j = it % nsl
            slot = slots[j]
            ssem = slot_sem(ld_sem, j)
            if oc == 0 and pre_wait is not None:
                wait_mark(ld_eng, [pre_wait])
            wait_mark(ld_eng, ring.prod_wait(it))
            src = wdram[:, oc * 128:(oc + 1) * 128].rearrange("(c p) n -> p c n", p=128)
            mld = dma(ld_eng, ssem, slot[:, :].rearrange("p (c n) -> p c n", n=128), src)
            if after_load is not None:
                after_load(oc)
            wait_mark('tensor', [mld])
            if mov_wait is not None and oc == 0:
                wait_mark('tensor', [mov_wait])
            mlast = None
            for tg in range(TGN):
                p = PROT[(oc * TGN + tg) % 4]
                use_bank(p)
                for k in range(CH - 1):
                    on('tensor', lambda t, p=p, slot=slot, k=k, tg=tg: t.matmul(
                        p[:, :], slot[:, k * 128:(k + 1) * 128], mov_ap(k, tg),
                        start=(k == 0), stop=False))
                mlast = sig('tensor', lambda t, p=p, slot=slot, tg=tg: t.matmul(
                    p[:, :], slot[:, (CH - 1) * 128:CH * 128], mov_ap(CH - 1, tg),
                    start=False, stop=True))
                mmM[id(p)] = mlast
                evac(oc, tg, p)
            ring.consumed(it, mlast)

    q_done = {}
    e_ring = Ring(2)
    e_it = [0]
    def elu_evac(oc, tg, p):
        mmn = mmM[id(p)]
        it = e_it[0]; e_it[0] += 1
        ms = scrA[it % 2]; esl = escr[it % 2]; rs = scrB[it % 2]
        nbq = nvsb[:, oc:oc + 1]
        bq = vsb[:, VOFF['bqk'] + oc:VOFF['bqk'] + oc + 1]
        if oc == 0 and tg == 0:
            wait_mark('scalar', [CONSTS_RDY])
        wait_mark('scalar', e_ring.prod_wait(it))
        wait_mark('scalar', [mmn])
        on('scalar', lambda s, p=p, ms=ms, nbq=nbq: s.activation(ms[:, :], p[:, :], AF.Relu, bias=nbq, scale=-1.0))
        if elu_mode >= 2:
            me = sig('scalar', lambda s, ms=ms, esl=esl: s.activation(esl[:, :], ms[:, :], AF.Exp, scale=-1.0))
        else:
            me = sig('scalar', lambda s, ms=ms, esl=esl: s.activation(esl[:, :], ms[:, :], AF.Copy))
        evM[id(p)].append(me)
        wait_mark('vector', [me])
        if elu_mode >= 3:
            mrs = sig('vector', lambda v, p=p, rs=rs, bq=bq: v.tensor_scalar(rs[:, :], p[:, :], bq, 0.0, ALU.add, ALU.max))
        else:
            mrs = sig('vector', lambda v, p=p, rs=rs: v.tensor_copy(rs[:, :], p[:, :]))
        evM[id(p)].append(mrs)
        dst = A1[:, oc * 2048 + tg * TG: oc * 2048 + (tg + 1) * TG]
        if elu_mode >= 4:
            mq = sig('vector', lambda v, rs=rs, esl=esl, dst=dst: v.scalar_tensor_tensor(
                dst, rs, -1.0, esl, ALU.add, ALU.add))
        else:
            mq = sig('vector', lambda v, rs=rs, dst=dst: v.tensor_copy(dst, rs[:, :]))
        q_done[(oc, tg)] = mq
        e_ring.consumed(it, mq)

    def simple_evac(oc, tg, p):
        mmn = mmM[id(p)]
        dst = A1[:, oc * 2048 + tg * TG: oc * 2048 + (tg + 1) * TG]
        wait_mark('scalar', [mmn])
        mact = sig('scalar', lambda s, p=p, dst=dst: s.activation(dst, p[:, :], AF.Copy))
        evM[id(p)].append(mact)
        q_done[(oc, tg)] = mact
    projfm(wqk, lambda k, tg: A2[:, k * 2048 + tg * TG: k * 2048 + (tg + 1) * TG],
           X1_ALL, simple_evac if simple_elu else elu_evac, "qk")
    if dbg:
        for c in range(CH):
            wait_mark('sync', [q_done[(c, tg)] for tg in range(TGN)])
            dma('sync', sDBG, dbg_t["dq"][c * 128:(c + 1) * 128, :], A1[:, c * 2048:(c + 1) * 2048])

    snap("p4")
    qtm_ring = Ring(2)
    for h in range(NH):
        pt = PT1 if h % 2 == 0 else PT2
        stage = qtmr[h % 2]
        wait_mark('tensor', [q_done[(h, tg)] for tg in range(TGN)])
        wait_mark('vector', qtm_ring.prod_wait(h))
        mcopy = None
        for half in range(2):
            use_bank(pt)
            for j in range(7):
                tt = half * 8 + j
                src = A1[:, h * 2048 + tt * 128: h * 2048 + (tt + 1) * 128]
                on('tensor', lambda t, pt=pt, j=j, src=src: t.transpose(
                    pt[:, j * 128:(j + 1) * 128], src, idbf[:, :]))
            tt = half * 8 + 7
            src = A1[:, h * 2048 + tt * 128: h * 2048 + (tt + 1) * 128]
            mtp = sig('tensor', lambda t, pt=pt, src=src: t.transpose(
                pt[:, 7 * 128:8 * 128], src, idbf[:, :]))
            wait_mark('vector', [mtp])
            mcopy = sig('vector', lambda v, pt=pt, stage=stage, half=half: v.tensor_copy(
                stage[:, half * 1024:(half + 1) * 1024], pt[:, :]))
            evM[id(pt)].append(mcopy)
        wait_mark('sync', [mcopy])
        ssp = slot_sem("SPQ", h % 2)
        msp = dma('sync', ssp, q_tm_sp[h * 128:(h + 1) * 128, :], stage[:, :])
        qtm_ring.consumed(h, msp)
    for c in range(CH):
        wait_mark('sync', [q_done[(c, tg)] for tg in range(TGN)])
        dma('sync', sQFS, q_fm_sp[c * 128:(c + 1) * 128, :], A1[:, c * 2048:(c + 1) * 2048])
    wait('sync', sQFS, sQFS.n)
    Y_QFM = notify('sync', nY)

    snap("p5")

    snap("p6")
    v_done = {}
    def v_evac(oc, tg, p):
        mmn = mmM[id(p)]
        bv = vsb[:, VOFF['bv'] + oc:VOFF['bv'] + oc + 1]
        dst = A1[:, oc * 2048 + tg * TG: oc * 2048 + (tg + 1) * TG]
        wait_mark('scalar', [mmn])
        mact = sig('scalar', lambda s, p=p, dst=dst, bv=bv: s.activation(dst, p[:, :], AF.Silu, bias=bv))
        evM[id(p)].append(mact)
        v_done[(oc, tg)] = mact
    projfm(wv, lambda k, tg: A2[:, k * 2048 + tg * TG: k * 2048 + (tg + 1) * TG],
           Y_QFM, v_evac, "v")
    if dbg:
        for c in range(CH):
            wait_mark('sync', [v_done[(c, tg)] for tg in range(TGN)])
            dma('sync', sDBG, dbg_t["dv"][c * 128:(c + 1) * 128, :], A1[:, c * 2048:(c + 1) * 2048])

    snap("p7")
    vtm_ring = Ring(2)
    vtm_ring.marks[0] = [X1_ALL]
    qtm_ld_ring = Ring(2)
    wait('sync', slot_sem("SPQ", 0), slot_sem("SPQ", 0).n)
    wait('sync', slot_sem("SPQ", 1), slot_sem("SPQ", 1).n)
    kv_done = []
    for h in range(NH):
        pt = PT1 if h % 2 == 0 else PT2
        vstage = vtmr[h % 2]
        qslot = qtmr[h % 2]
        wait_mark('sync', qtm_ld_ring.prod_wait(h))
        sql = slot_sem("QTL", h % 2)
        mq = dma('sync', sql, qslot[:, :], q_tm_sp[h * 128:(h + 1) * 128, :])
        wait_mark('tensor', [v_done[(h, tg)] for tg in range(TGN)])
        wait_mark('vector', vtm_ring.prod_wait(h))
        for half in range(2):
            use_bank(pt)
            for j in range(7):
                tt = half * 8 + j
                src = A1[:, h * 2048 + tt * 128: h * 2048 + (tt + 1) * 128]
                on('tensor', lambda t, pt=pt, j=j, src=src: t.transpose(
                    pt[:, j * 128:(j + 1) * 128], src, idbf[:, :]))
            tt = half * 8 + 7
            src = A1[:, h * 2048 + tt * 128: h * 2048 + (tt + 1) * 128]
            mtp = sig('tensor', lambda t, pt=pt, src=src: t.transpose(
                pt[:, 7 * 128:8 * 128], src, idbf[:, :]))
            wait_mark('vector', [mtp])
            mcp = sig('vector', lambda v, pt=pt, vstage=vstage, half=half: v.tensor_copy(
                vstage[:, half * 1024:(half + 1) * 1024], pt[:, :]))
            evM[id(pt)].append(mcp)
        p = PROT[h // 4]
        if h % 4 == 0:
            use_bank(p)
        wait_mark('tensor', [mcp, mq])
        for tt in range(TTN - 1):
            on('tensor', lambda t, p=p, h=h, tt=tt, qslot=qslot, vstage=vstage: t.matmul(
                p[:, (h % 4) * 128:(h % 4 + 1) * 128],
                qslot[:, tt * 128:(tt + 1) * 128],
                vstage[:, tt * 128:(tt + 1) * 128],
                start=(tt == 0), stop=False))
        mkv = sig('tensor', lambda t, p=p, h=h, qslot=qslot, vstage=vstage: t.matmul(
            p[:, (h % 4) * 128:(h % 4 + 1) * 128],
            qslot[:, (TTN - 1) * 128:TTN * 128],
            vstage[:, (TTN - 1) * 128:TTN * 128],
            start=False, stop=True))
        kv_done.append(mkv)
        qtm_ld_ring.consumed(h, mkv)
        vtm_ring.consumed(h, mkv)
    KV_ALL = kv_done[-1]
    wait_mark('vector', [KV_ALL])
    mevkv = None
    for i in range(4):
        mevkv = sig('vector', lambda v, i=i: v.tensor_copy(kvacc[:, i * 512:(i + 1) * 512], PROT[i][:, :]))
        evM[id(PROT[i])].append(mevkv)
    wait_mark('gpsimd', [mevkv])
    dma('gpsimd', sG, cc_in[:, :], kvacc[:, :])
    wait('gpsimd', sG, sG.n)
    on('gpsimd', lambda g: g.collective_compute(
        "AllReduce", ALU.add, replica_groups=[list(range(NCORES))],
        ins=[cc_in[:, :]], outs=[cc_out[:, :]]).then_inc(sAR.h, 1))
    sAR.bump()
    sPWO = Sem(nc, es, "sPWO")
    u_stage_ring = Ring(2)
    u_stage_ring.seed_all(KV_ALL)
    u_sp_marks = []
    def u_evac(oc, tg, p):
        mmn = mmM[id(p)]
        stage = urng[oc % 2]
        bu = vsb[:, VOFF['bu'] + oc:VOFF['bu'] + oc + 1]
        if tg == 0:
            wait_mark('scalar', u_stage_ring.prod_wait(oc))
        wait_mark('scalar', [mmn])
        mact = sig('scalar', lambda s, p=p, stage=stage, tg=tg, bu=bu: s.activation(
            stage[:, tg * TG:(tg + 1) * TG], p[:, :], AF.Silu, bias=bu))
        evM[id(p)].append(mact)
        if tg == TGN - 1:
            wait_mark('sync', [mact])
            ssp = slot_sem("SPU", oc % 2)
            msp = dma('sync', ssp, u_sp[oc * 128:(oc + 1) * 128, :], stage[:, :])
            u_sp_marks.append(msp)
            u_stage_ring.consumed(oc, msp)
    def u_after_load(oc):
        if oc == 2:
            for i in range(4):
                dma('gpsimd', sPWO, wo_c[i * 512:(i + 1) * 512, :], wo[i * 512:(i + 1) * 512, :])
    projfm(wu, lambda k, tg: A2[:, k * 2048 + tg * TG: k * 2048 + (tg + 1) * TG],
           None, u_evac, "u", after_load=u_after_load)
    wait('gpsimd', sPWO, sPWO.n)
    PREWO = notify('gpsimd', nG)
    if dbg:
        for c in range(CH):
            wait_mark('sync', [u_sp_marks[c]])
            dma('sync', sDBG, dbg_t["du"][c * 128:(c + 1) * 128, :], u_sp[c * 128:(c + 1) * 128, :])
    wait('gpsimd', sAR, 1)
    wait_mark('gpsimd', wslot_ring.marks[0] + wslot_ring.marks[1])
    dma('gpsimd', sG, kv_bf[:, :], cc_out[:, :])
    wait('gpsimd', sG, sG.n)
    GKV = notify('gpsimd', nG)
    sPW0, sPW1 = Sem(nc, es, "sPW0"), Sem(nc, es, "sPW1")
    def _prestage_items():
        for i in range(32):
            yield (sPW0, w0c[i * 64:(i + 1) * 64, :], w0t[i * 64:(i + 1) * 64, :])
        for i in range(32):
            yield (sPW1, w1c[i * 256:(i + 1) * 256, :], w1t[i * 256:(i + 1) * 256, :])
    _pre_gen = _prestage_items()
    _pre_hist = []
    def emit_prestage_chunk(k):
        for _ in range(k):
            item = next(_pre_gen, None)
            if item is None:
                return
            sem, dst, src = item
            if len(_pre_hist) >= 3:
                wait('gpsimd', *_pre_hist[-3])
            _pre_hist.append(dma('gpsimd', sem, dst, src))
    if dbg:
        wait_mark('sync', [GKV])
        dma('sync', sDBG, dbg_t["dkv"][:, :], cc_out[:, :])

    snap("p8")
    qst4 = [qstr[0], qstr[1], urng[0], urng[1]]
    qst_ring = Ring(4)
    qst_ring.marks[0] = [KV_ALL]
    qst_ring.marks[1] = [KV_ALL]
    qst_ring.marks[2] = [u_sp_marks[14], KV_ALL]
    qst_ring.marks[3] = [u_sp_marks[15], KV_ALL]
    out_done = {}
    for h in range(NH):
        qslot = qst4[h % 4]
        wait_mark('sync', qst_ring.prod_wait(h))
        sql = slot_sem("QSL", h % 4)
        mq = dma('sync', sql, qslot[:, :], q_fm_sp[h * 128:(h + 1) * 128, :])
        wait_mark('tensor', [mq])
        if h == 0:
            wait_mark('tensor', [GKV])
        for tg in range(TGN):
            p = PROT[(h * TGN + tg) % 4]
            use_bank(p)
            mmm = sig('tensor', lambda t, p=p, h=h, tg=tg, qslot=qslot: t.matmul(
                p[:, :], kv_bf[:, h * 128:(h + 1) * 128],
                qslot[:, tg * TG:(tg + 1) * TG], start=True, stop=True))
            dst = A2[:, h * 2048 + tg * TG: h * 2048 + (tg + 1) * TG]
            wait_mark('scalar', [mmm])
            mev = sig('scalar', lambda s, p=p, dst=dst: s.activation(dst, p[:, :], AF.Copy))
            evM[id(p)].append(mev)
            out_done[(h, tg)] = mev
        qst_ring.consumed(h, out_done[(h, TGN - 1)])
    OUT_ALL = out_done[(NH - 1, TGN - 1)]
    wait_mark('gpsimd', [OUT_ALL])
    emit_prestage_chunk(200)
    if dbg:
        for c in range(CH):
            wait_mark('sync', [OUT_ALL])
            dma('sync', sDBG, dbg_t["dout"][c * 128:(c + 1) * 128, :], A2[:, c * 2048:(c + 1) * 2048])

    snap("p9")
    def osrc(c, tg):
        return A2[:, c * 2048 + tg * TG: c * 2048 + (tg + 1) * TG]
    ROWS_B = stats(osrc, range(TGN), TG, r_bf, s_bf, src_ready=lambda c: OUT_ALL)

    wait('sync', slot_sem("SPU", 0), slot_sem("SPU", 0).n)
    wait('sync', slot_sem("SPU", 1), slot_sem("SPU", 1).n)
    u_ld_ring = Ring(2)
    u_ld_ring.marks[0] = [u_sp_marks[14], out_done[(14, TGN - 1)]]
    u_ld_ring.marks[1] = [u_sp_marks[15], out_done[(15, TGN - 1)]]
    rB_b = at("rB_b", [128, 2048], BF16, 163840)
    sB_b = at("sB_b", [128, 2048], BF16, 139264)
    for tg in range(TGN):
        mb = bcast(PS1, r_bf[0:1, tg * TG:(tg + 1) * TG], TG,
                   extra=ROWS_B if tg == 0 else None)
        wait_mark('vector', [mb])
        if tg == 0:
            wait_mark('vector', [OUT_ALL])
        er = sig('vector', lambda v, tg=tg: v.tensor_copy(rB_b[:, tg * TG:(tg + 1) * TG], PS1[:, :TG]))
        evM[id(PS1)].append(er)
        mb2 = bcast(PS2, s_bf[0:1, tg * TG:(tg + 1) * TG], TG)
        wait_mark('vector', [mb2])
        esv = sig('vector', lambda v, tg=tg: v.tensor_copy(sB_b[:, tg * TG:(tg + 1) * TG], PS2[:, :TG]))
        evM[id(PS2)].append(esv)
    z_done = {}
    for oc in range(NH):
        uslot = urng[oc % 2]
        wait_mark('sync', u_ld_ring.prod_wait(oc))
        sul = slot_sem("UL", oc % 2)
        mu = dma('sync', sul, uslot[:, :], u_sp[oc * 128:(oc + 1) * 128, :])
        gcol = vsb[:, VOFF['g'] + oc:VOFF['g'] + oc + 1]
        bcol = vsb[:, VOFF['b'] + oc:VOFF['b'] + oc + 1]
        for tg in range(TGN):
            t1 = scrA[tg % 2]; t2 = scrB[tg % 2]
            src = A2[:, oc * 2048 + tg * TG: oc * 2048 + (tg + 1) * TG]
            on('vector', lambda v, src=src, t1=t1, tg=tg: v.tensor_tensor(
                t1[:, :], src, rB_b[:, tg * TG:(tg + 1) * TG], ALU.mult))
            on('vector', lambda v, t1=t1, t2=t2, tg=tg: v.tensor_tensor(
                t2[:, :], t1[:, :], sB_b[:, tg * TG:(tg + 1) * TG], ALU.add))
            on('vector', lambda v, t2=t2, t1=t1, gcol=gcol, bcol=bcol: v.tensor_scalar(
                t1[:, :], t2[:, :], gcol, bcol, ALU.mult, ALU.add))
            wait_mark('vector', [mu])
            mz = sig('vector', lambda v, t1=t1, uslot=uslot, tg=tg, src=src: v.tensor_tensor(
                src, t1[:, :], uslot[:, tg * TG:(tg + 1) * TG], ALU.mult))
            z_done[(oc, tg)] = mz
        u_ld_ring.consumed(oc, z_done[(oc, TGN - 1)])
    Z_ALL = z_done[(NH - 1, TGN - 1)]
    if dbg:
        for c in range(CH):
            wait_mark('sync', [Z_ALL])
            dma('sync', sDBG, dbg_t["dz"][c * 128:(c + 1) * 128, :], A2[:, c * 2048:(c + 1) * 2048])

    snap("p10")
    wo_slots = [wrng[0], wrng[1], w0rng[2], w0rng[3]]
    wo_ring = Ring(4)
    wo_ring.marks[0] = list(wslot_ring.marks[0]) + [OUT_ALL]
    wo_ring.marks[1] = list(wslot_ring.marks[1]) + [OUT_ALL]
    wo_ring.marks[2] = [OUT_ALL]
    wo_ring.marks[3] = [OUT_ALL]
    wo_it = [0]
    a_done = {}
    def a_evac(oc, tg, p):
        mmn = mmM[id(p)]
        bo = vsb[:, VOFF['bo'] + oc:VOFF['bo'] + oc + 1]
        dst = A1[:, oc * 2048 + tg * TG: oc * 2048 + (tg + 1) * TG]
        wait_mark('scalar', [mmn])
        mact = sig('scalar', lambda s, p=p, dst=dst, bo=bo: s.activation(dst, p[:, :], AF.Identity, bias=bo))
        evM[id(p)].append(mact)
        a_done[(oc, tg)] = mact
    projfm(wo_c, lambda k, tg: A2[:, k * 2048 + tg * TG: k * 2048 + (tg + 1) * TG],
           Z_ALL, a_evac, "wo", slots=wo_slots, ring=wo_ring, itc=wo_it,
           ld_eng='sync', ld_sem='WLY', pre_wait=PREWO)
    wait('gpsimd', sPW0, sPW0.n)
    PRE0 = notify('gpsimd', nG)
    wait('gpsimd', sPW1, sPW1.n)
    PRE1 = notify('gpsimd', nG)
    A_ALL = a_done[(CH - 1, TGN - 1)]
    if dbg:
        for c in range(CH):
            wait_mark('sync', [A_ALL])
            dma('sync', sDBG, dbg_t["da"][c * 128:(c + 1) * 128, :], A1[:, c * 2048:(c + 1) * 2048])

    snap("p11")
    def asrc(c, tg):
        return A1[:, c * 2048 + tg * TG: c * 2048 + (tg + 1) * TG]
    ROWS_C = stats(asrc, range(TGN), TG, r_bf, s_bf, src_ready=lambda c: A_ALL)
    sBt = at("sBt", [128, 2048], BF16, 176128)
    sb_marks = []
    for tg in range(TGN):
        mb = bcast(PS1, r_bf[0:1, tg * TG:(tg + 1) * TG], TG,
                   extra=ROWS_C if tg == 0 else None)
        wait_mark('vector', [mb])
        if tg == 0:
            wait_mark('vector', [Z_ALL])
        e1 = sig('vector', lambda v, tg=tg: v.tensor_copy(rBt[:, tg * TG:(tg + 1) * TG], PS1[:, :TG]))
        evM[id(PS1)].append(e1)
        mb2 = bcast(PS2, s_bf[0:1, tg * TG:(tg + 1) * TG], TG)
        wait_mark('vector', [mb2])
        e2s = sig('vector', lambda v, tg=tg: v.tensor_copy(sBt[:, tg * TG:(tg + 1) * TG], PS2[:, :TG]))
        evM[id(PS2)].append(e2s)
        sb_marks.append(e2s)
    x_ring = Ring(2)
    y1_ring = Ring(2)
    last_y1w = None
    y1_vdone = None
    y1c_marks = []
    for c in range(CH):
        for half in range(2):
            i2 = c * 2 + half
            wait_mark('sync', x_ring.prod_wait(i2))
            sxr = slot_sem("XR", i2 % 2)
            mx = dma('sync', sxr, xrng[i2 % 2][:, :],
                     xT[c * 128:(c + 1) * 128, half * 1024:(half + 1) * 1024])
            if i2 == 0:
                wait_mark('vector', sb_marks)
                wait_mark('scalar', [A_ALL])
            on('vector', lambda v, c=c, half=half: v.tensor_tensor(
                ctscr[:, half * 1024:(half + 1) * 1024],
                A1[:, c * 2048 + half * 1024: c * 2048 + (half + 1) * 1024],
                rBt[:, half * 1024:(half + 1) * 1024], ALU.mult))
            on('vector', lambda v, half=half: v.tensor_tensor(
                ctscr[:, half * 1024:(half + 1) * 1024], ctscr[:, half * 1024:(half + 1) * 1024],
                sBt[:, half * 1024:(half + 1) * 1024], ALU.add))
            wait_mark('vector', [mx])
            wait_mark('vector', y1_ring.prod_wait(i2))
            yslot = y1rng[i2 % 2]
            my1 = sig('vector', lambda v, half=half, yslot=yslot, i2=i2: v.tensor_tensor(
                yslot[:, :], ctscr[:, half * 1024:(half + 1) * 1024],
                xrng[i2 % 2][:, :], ALU.add))
            x_ring.consumed(i2, my1)
            wait_mark('scalar', [my1])
            mcp = sig('scalar', lambda sc, yslot=yslot, c=c, half=half: sc.activation(
                A2[:, c * 2048 + half * 1024: c * 2048 + (half + 1) * 1024], yslot[:, :], AF.Copy))
            if half == 1:
                y1c_marks.append(mcp)
            y1_vdone = mcp
            wait_mark('sync', [my1])
            syw = slot_sem("Y1W", i2 % 2)
            mw = dma('sync', syw, y1_sp[c * 128:(c + 1) * 128, half * 1024:(half + 1) * 1024],
                     yslot[:, :])
            y1_ring.consumed(i2, mw, mcp)
            last_y1w = mw
    Y1_V = y1_vdone
    if dbg:
        wait('sync', slot_sem("Y1W", 0), slot_sem("Y1W", 0).n)
        wait('sync', slot_sem("Y1W", 1), slot_sem("Y1W", 1).n)
        for c in range(CH):
            dma('sync', sDBG, dbg_t["dy1"][c * 128:(c + 1) * 128, :], y1_sp[c * 128:(c + 1) * 128, :])

    snap("p12")
    def y1src(c, tg):
        return A2[:, c * 2048 + tg * TG: c * 2048 + (tg + 1) * TG]
    ROWS_D = stats(y1src, range(TGN), TG, rd_bf, sd_bf, src_ready=lambda c: y1c_marks[c])

    snap("p13")
    wait('sync', slot_sem("Y1W", 0), slot_sem("Y1W", 0).n)
    wait('sync', slot_sem("Y1W", 1), slot_sem("Y1W", 1).n)
    YW_ALL = notify('sync', nY)
    w0_ring = Ring(4)
    for _j in range(4):
        w0_ring.marks[_j] = list(wo_ring.marks[_j]) + [Y1_V]
    w1_ring = Ring(2)
    w1_ring.seed_all(Y1_V, KV_ALL)
    fy1_ring = Ring(2)
    fy2_ring = Ring(2)
    fy2_ring.seed_all(last_y1w)
    fy1_it = [0]; fy2_it = [0]
    w0_it = [0]; w1_it = [0]
    last_out = None
    def x2_emit(bb, prior_ffn1_mark):
        tok0b = bb * FT
        use_bank(PD)
        m1 = sig('tensor', lambda t, tok0b=tok0b: t.matmul(
            PD[:, :FT], ones_row[0:1, 0:128], rd_bf[0:1, tok0b:tok0b + FT], start=True, stop=True))
        wait_mark('vector', [m1])
        e1 = sig('vector', lambda v: v.tensor_copy(rdBb[:, :], PD[:, :FT]))
        evM[id(PD)].append(e1)
        use_bank(PD)
        m2 = sig('tensor', lambda t, tok0b=tok0b: t.matmul(
            PD[:, :FT], ones_row[0:1, 0:128], sd_bf[0:1, tok0b:tok0b + FT], start=True, stop=True))
        wait_mark('vector', [m2])
        e2 = sig('vector', lambda v: v.tensor_copy(sdBb[:, :], PD[:, :FT]))
        evM[id(PD)].append(e2)
        mx2 = None
        for c in range(CH):
            it = fy1_it[0]; fy1_it[0] += 1
            wait_mark('sync', fy1_ring.prod_wait(it))
            sf = slot_sem("FY1", it % 2)
            my = dma('sync', sf, fy1rng[it % 2][:, :],
                     y1_sp[c * 128:(c + 1) * 128, tok0b:tok0b + FT])
            wait_mark('vector', [my])
            if c == 0 and prior_ffn1_mark is not None:
                wait_mark('vector', [prior_ffn1_mark])
            t1 = scrA[c % 2]
            on('vector', lambda v, it=it, t1=t1: v.tensor_tensor(t1[:, :FT], fy1rng[it % 2][:, :], rdBb[:, :], ALU.mult))
            mx2 = sig('vector', lambda v, c=c, t1=t1: v.tensor_tensor(
                A2[:, c * 512:(c + 1) * 512], t1[:, :FT], sdBb[:, :], ALU.add))
            fy1_ring.consumed(it, mx2)
        return mx2

    w0_loads = {}
    w0_ptr = [0]
    def w0_emit_load():
        idx = w0_ptr[0]
        if idx >= FB * NFC:
            return
        w0_ptr[0] += 1
        ncn = idx % NFC
        j = idx % 4
        slot = w0rng[j]
        if idx == 0:
            wait_mark('gpsimd', [PRE0])
        wait_mark('gpsimd', w0_ring.prod_wait(idx))
        mld = dma('gpsimd', slot_sem("W0G", j),
                  slot[:, :].rearrange("p (c n) -> p c n", n=128),
                  w0c[:, ncn * 128:(ncn + 1) * 128].rearrange("(c p) n -> p c n", p=128))
        w0_loads[idx] = (mld, slot)

    X2_MARKS = {}
    for b in range(FB):
        tok0 = b * FT
        if b == 0:
            wait_mark('tensor', [ROWS_D])
            X2_MARKS[0] = x2_emit(0, None)
        X2_ALL = X2_MARKS[b]
        if dbg and b == dbgb:
            wait_mark('sync', [X2_ALL])
            dma('sync', sDBG, dbg_t["dx2"][:, :], A2[:, 0:8192])

        for ncn in range(NFC):
            idx = b * NFC + ncn
            while w0_ptr[0] <= min(idx + 3, FB * NFC - 1):
                w0_emit_load()
            mld, slot = w0_loads[idx]
            p = PROT[ncn % 4]
            use_bank(p)
            wait_mark('tensor', [mld])
            if ncn == 0:
                wait_mark('tensor', [X2_ALL])
            for k in range(CH - 1):
                on('tensor', lambda t, p=p, slot=slot, k=k: t.matmul(
                    p[:, :], slot[:, k * 128:(k + 1) * 128],
                    A2[:, k * 512:(k + 1) * 512], start=(k == 0), stop=False))
            mmm = sig('tensor', lambda t, p=p, slot=slot: t.matmul(
                p[:, :], slot[:, (CH - 1) * 128:CH * 128],
                A2[:, (CH - 1) * 512:CH * 512], start=False, stop=True))
            w0_ring.consumed(idx, mmm)
            if idx + 4 <= FB * NFC - 1 and w0_ptr[0] == idx + 4:
                w0_emit_load()
            if ncn == NFC - 1:
                ffn1_last = mmm
            b0c = vsb[:, VOFF['b0'] + ncn:VOFF['b0'] + ncn + 1]
            wait_mark('scalar', [mmm])
            if ncn == 0:
                wait_mark('scalar', [X2_ALL])
            mev = sig('scalar', lambda s, p=p, ncn=ncn, b0c=b0c: s.activation(
                A1[:, ncn * 512:(ncn + 1) * 512], p[:, :], AF.Gelu, bias=b0c))
            evM[id(p)].append(mev)
            if ncn == NFC - 1:
                D_ALL = mev
        if b == dbgb:
            if dbg:
                wait_mark('sync', [D_ALL])
                dma('sync', sDBG, dbg_t["dD"][:, :], A1[:, 0:32768])

        if b + 1 < FB:
            X2_MARKS[b + 1] = x2_emit(b + 1, ffn1_last)
        lne_pend = []
        lne_mm_last = [None]
        def _flush_lne_stats():
            if not lne_pend:
                return
            hc0, sqslot, msq, it_sq = lne_pend.pop(0)
            wait_mark('tensor', [msq])
            on('tensor', lambda t, hc0=hc0: t.matmul(
                PS1[0:1, :FT], onesc[:, :], A2[:, 8192 + hc0 * 512: 8192 + (hc0 + 1) * 512],
                start=(hc0 == 0), stop=(hc0 == CH - 1)))
            mc = sig('tensor', lambda t, sqslot=sqslot, hc0=hc0: t.matmul(
                PS2[0:1, :FT], onesc[:, :], sqslot[:, :FT],
                start=(hc0 == 0), stop=(hc0 == CH - 1)))
            sq_ring.consumed(it_sq, mc)
            lne_mm_last[0] = mc
        for hc in range(CH):
            def w1_load(pc, hc=hc):
                it = w1_it[0]; w1_it[0] += 1
                j = it % 2
                off = j * 2048
                if b == 0 and hc == 0 and pc == 0:
                    wait_mark('sync', [PRE1])
                wait_mark('sync', w1_ring.prod_wait(it))
                mld = dma('sync', slot_sem("W1LY", j),
                          w1rng[:, off:off + 2048].rearrange("p (m h) -> p m h", h=128),
                          w1t[pc * 16 * 128:(pc + 1) * 16 * 128, hc * 128:(hc + 1) * 128]
                          .rearrange("(m p) h -> p m h", p=128) if False else
                          w1c[pc * 16 * 128:(pc + 1) * 16 * 128, hc * 128:(hc + 1) * 128]
                          .rearrange("(m p) h -> p m h", p=128))
                return (mld, off, it, None)
            pending = [w1_load(0), w1_load(1)]
            p = PROT[hc % 3]
            use_bank(p)
            if hc == 0:
                wait_mark('tensor', [D_ALL])
            mmm = None
            for pc in range(4):
                mld, off, it, mwb = pending[pc]
                wait_mark('tensor', [mld])
                for j2 in range(15):
                    ncn = pc * 16 + j2
                    on('tensor', lambda t, p=p, off=off, j2=j2, ncn=ncn: t.matmul(
                        p[:, :], w1rng[:, off + j2 * 128: off + (j2 + 1) * 128],
                        A1[:, ncn * 512:(ncn + 1) * 512],
                        start=(ncn == 0), stop=False))
                ncn = pc * 16 + 15
                mmm = sig('tensor', lambda t, p=p, off=off, ncn=ncn: t.matmul(
                    p[:, :], w1rng[:, off + 15 * 128: off + 16 * 128],
                    A1[:, ncn * 512:(ncn + 1) * 512],
                    start=False, stop=(ncn == NFC - 1)))
                w1_ring.consumed(it, mmm, mwb)
                if pc + 2 < 4:
                    pending.append(w1_load(pc + 2))
            b1c = vsb[:, VOFF['b1'] + hc:VOFF['b1'] + hc + 1]
            wait_mark('scalar', [mmm])
            if hc == 0:
                wait_mark('scalar', [X2_ALL])
            mev = sig('scalar', lambda s, p=p, hc=hc, b1c=b1c: s.activation(
                A2[:, 8192 + hc * 512: 8192 + (hc + 1) * 512], p[:, :], AF.Identity, bias=b1c))
            evM[id(p)].append(mev)
            if hc == 0:
                use_bank(PS1); use_bank(PS2)
            it_sq = sq_it[0]; sq_it[0] += 1
            sqslot = SQ4[it_sq % 4]
            wait_mark('scalar', sq_ring.prod_wait(it_sq))
            msq = sig('scalar', lambda s, sqslot=sqslot, hc=hc: s.square(
                sqslot[:, :FT], A2[:, 8192 + hc * 512: 8192 + (hc + 1) * 512]))
            lne_pend.append((hc, sqslot, msq, it_sq))
            if hc >= 1:
                _flush_lne_stats()
            if hc == CH - 1:
                E_ALL = mev
        _flush_lne_stats()
        _flush_lne_stats()
        if b == dbgb:
            if dbg:
                wait_mark('sync', [E_ALL])
                dma('sync', sDBG, dbg_t["dE"][:, :], A2[:, 8192:16384])

        mm_e = lne_mm_last[0]
        wait_mark('vector', [mm_e])
        e1 = sig('vector', lambda v: v.tensor_copy(s1t[:, :FT], PS1[0:1, :FT]))
        e2 = sig('vector', lambda v: v.tensor_copy(s2t[:, :FT], PS2[0:1, :FT]))
        evM[id(PS1)].append(e1)
        evM[id(PS2)].append(e2)
        on('vector', lambda v: v.tensor_scalar(s1t[:, :FT], s1t[:, :FT], 1.0 / D, None, ALU.mult))
        on('vector', lambda v: v.tensor_tensor(tmpt[:, :FT], s1t[:, :FT], s1t[:, :FT], ALU.mult))
        on('vector', lambda v: v.tensor_scalar(s2t[:, :FT], s2t[:, :FT], 1.0 / D, EPS, ALU.mult, ALU.add))
        on('vector', lambda v: v.tensor_tensor(s2t[:, :FT], s2t[:, :FT], tmpt[:, :FT], ALU.subtract))
        mrec = sig('vector', lambda v: v.reciprocal(s2t[:, :FT], s2t[:, :FT]))
        wait_mark('scalar', [mrec])
        mr = sig('scalar', lambda s: s.activation(r_bf[0:1, :FT], s2t[:, :FT], AF.Sqrt))
        wait_mark('vector', [mr])
        ROWS_E = sig('vector', lambda v: v.scalar_tensor_tensor(
            s_bf[0:1, :FT], s1t[:, :FT], -1.0, r_bf[0:1, :FT], ALU.mult, ALU.mult))
        mbr = bcast(PS1, r_bf[0:1, :FT], FT, extra=ROWS_E)
        mbs = bcast(PS2, s_bf[0:1, :FT], FT)
        ln_e_last = None
        for c in range(CH):
            it = fy1_it[0]; fy1_it[0] += 1
            wait_mark('sync', fy1_ring.prod_wait(it))
            sf = slot_sem("FY1", it % 2)
            my = dma('sync', sf, fy1rng[it % 2][:, :],
                     y1_sp[c * 128:(c + 1) * 128, tok0:tok0 + FT])
            if c == 0:
                wait_mark('vector', [mbr, mbs])
            t1 = scrA[c % 2]
            on('vector', lambda v, c=c, t1=t1: v.tensor_tensor(
                t1[:, :FT], A2[:, 8192 + c * 512: 8192 + (c + 1) * 512], PS1[:, :FT], ALU.mult))
            on('vector', lambda v, t1=t1: v.tensor_tensor(t1[:, :FT], t1[:, :FT], PS2[:, :FT], ALU.add))
            wait_mark('vector', [my])
            it2 = fy2_it[0]; fy2_it[0] += 1
            wait_mark('vector', fy2_ring.prod_wait(it2))
            yslot = fy2rng[it2 % 2]
            my2 = sig('vector', lambda v, t1=t1, it=it, yslot=yslot: v.tensor_tensor(
                yslot[:, :], t1[:, :FT], fy1rng[it % 2][:, :], ALU.add))
            fy1_ring.consumed(it, my2)
            ln_e_last = my2
            wait_mark('sync', [my2])
            so = slot_sem("OUT", it2 % 2)
            mo = dma('sync', so, yT[c * 128:(c + 1) * 128, tok0:tok0 + FT], yslot[:, :])
            fy2_ring.consumed(it2, mo)
            last_out = mo
        evM[id(PS1)].append(ln_e_last)
        evM[id(PS2)].append(ln_e_last)

    wait('sync', slot_sem("OUT", 0), slot_sem("OUT", 0).n)
    wait('sync', slot_sem("OUT", 1), slot_sem("OUT", 1).n)
    if dbg:
        wait('sync', sDBG, sDBG.n)
    snap("all")

    if stop_after is not None:
        cut = phase_snaps[stop_after]
        for e in q:
            q[e] = q[e][:cut[e]]

    block = es.enter_context(nc.Block())
    with es:
        for eng in ("tensor", "vector", "scalar", "gpsimd", "sync"):
            fns = q[eng]
            def body(e, fns=fns):
                for fn in fns:
                    fn(e)
            getattr(block, eng)(body)
    return nc


def make_in_maps(ins):
    x = np.asarray(ins['L0_attn_x'])[0]
    xT_full = np.ascontiguousarray(x.T)
    vecs = pack_vecs(ins)
    grow = np.stack([np.asarray(ins['L0_attn_ln_g'], np.float32),
                     np.asarray(ins['L0_attn_ln_b'], np.float32)])
    w0tA = np.ascontiguousarray(np.asarray(ins['L0_ffn_W0'], np.float32).T)
    w1tA = np.ascontiguousarray(np.asarray(ins['L0_ffn_W1'], np.float32).T)
    idnt = np.eye(128, dtype=np.float32)
    onesr = np.ones((1, TG), np.float32)
    in_maps = []
    for c in range(NCORES):
        in_maps.append({
            "xT": np.ascontiguousarray(xT_full[:, c * T:(c + 1) * T]),
            "wqk": np.asarray(ins['L0_attn_Wqk'], np.float32),
            "wv": np.asarray(ins['L0_attn_Wv'], np.float32),
            "wu": np.asarray(ins['L0_attn_Wu'], np.float32),
            "wo": np.asarray(ins['L0_attn_Wo'], np.float32),
            "w0t": w0tA, "w1t": w1tA, "vecs": vecs, "grow": grow,
            "idnt": idnt, "onesr": onesr,
        })
    return in_maps


def kernel(**ins):
    nc = build_nc(dbg=False)
    in_maps = make_in_maps(ins)
    res = run_bass_kernel_spmd(nc, in_maps, core_ids=list(range(NCORES)))
    out = np.concatenate([res.results[c]["yT"] for c in range(NCORES)], axis=1)
    return np.ascontiguousarray(out.T)[None].astype(np.float32)
